# version 34
# baseline (speedup 1.0000x reference)
import contextlib
import numpy as np
import ml_dtypes
import concourse.bass as bass
from concourse.aot_env import allow_ici_all_to_all
import concourse.mybir as mybir
from concourse.bass_utils import run_bass_kernel_spmd

F32 = mybir.dt.float32
BF16 = mybir.dt.bfloat16
AF = mybir.ActivationFunctionType
ALU = mybir.AluOpType
AX = mybir.AxisListType

NC_ = 8

FULL = dict(B=2, L=16384, DM=2048, DI=4096, DS=16, DTR=128, HFF=5632, LC=1024,
            PSF=512)
MINI = dict(B=2, L=1024, DM=256, DI=2048, DS=16, DTR=128, HFF=1280, LC=256,
            PSF=128)


class Buf:
    __slots__ = ("writer", "readers")

    def __init__(self):
        self.writer = None
        self.readers = []


class Sched:
    ENGINES = ("sync", "tensor", "vector", "scalar", "gpsimd")
    DMA_SEMS = {"dld": "sync", "dst": "gpsimd", "dgp": "gpsimd", "cc": "gpsimd",
                "dx": "sync", "dw": "sync"}

    def __init__(self, nc, stack):
        self.nc = nc
        self.q = {e: [] for e in self.ENGINES}
        self.sems = {}
        self.cnt = {}
        self.memo = {e: {} for e in self.ENGINES}
        self.bufs = {}
        self.pub = {}
        for s in ("pe", "dve", "act", "pool", "dld", "dst", "dgp", "cc",
                  "dx", "dw", "dld_r", "dst_r", "dgp_r", "cc_r", "dx_r", "dw_r"):
            self.sems[s] = stack.enter_context(nc.semaphore(s))
            self.cnt[s] = 0
        for s in self.DMA_SEMS:
            self.pub[s] = (0, 0)

    def publish(self, s, upto):
        cov, rv = self.pub[s]
        if cov >= upto:
            return (s + "_r", rv)
        eng = self.DMA_SEMS[s]
        tot = self.cnt[s]
        self._wait(eng, s, tot)
        rs = s + "_r"
        self.cnt[rs] += 1
        h = self.sems[rs]
        v = self.cnt[rs]
        self.q[eng].append(lambda e, h=h: e.sem_inc(h, 1))
        self.pub[s] = (tot, v)
        return (rs, v)

    def buf(self, key):
        if key not in self.bufs:
            self.bufs[key] = Buf()
        return self.bufs[key]

    def _wait(self, eng, sem, val):
        if val <= 0 or self.memo[eng].get(sem, 0) >= val:
            return
        self.memo[eng][sem] = val
        h = self.sems[sem]
        self.q[eng].append(lambda e, h=h, v=val: e.wait_ge(h, v))

    def op(self, eng, build, ins=(), outs=(), sem="", amt=1, extra=()):
        if sem == "":
            sem = {"tensor": "pe", "vector": "dve", "scalar": "act",
                   "gpsimd": "pool"}.get(eng)
        deps = {}

        def add(d):
            if d is None:
                return
            if d[0] in self.DMA_SEMS:
                d = self.publish(d[0], d[1])
            if deps.get(d[0], 0) < d[1]:
                deps[d[0]] = d[1]
        for k in ins:
            add(self.buf(k).writer)
        for k in outs:
            b = self.buf(k)
            add(b.writer)
            for r in b.readers:
                add(r)
        for d in extra:
            add(d)
        if eng == "tensor":
            deps.pop("pe", None)
        for s, v in deps.items():
            self._wait(eng, s, v)
        h = self.sems[sem] if sem else None
        if h is not None:
            self.cnt[sem] += amt
            self.q[eng].append(lambda e, b=build, h=h, a=amt: b(e).then_inc(h, a))
            tok = (sem, self.cnt[sem])
        else:
            self.q[eng].append(lambda e, b=build: b(e))
            tok = None
        for k in outs:
            b = self.buf(k)
            b.writer = tok
            b.readers = []
        if tok:
            for k in ins:
                self.buf(k).readers.append(tok)
        return tok

    def dma(self, eng, out_ap, in_ap, ins=(), outs=(), sem="", extra=(), **kw):
        if sem == "":
            sem = {"sync": "dld", "gpsimd": "dgp"}.get(eng, "dst")
        eng = self.DMA_SEMS[sem]
        return self.op(eng, lambda e: e.dma_start(out=out_ap, in_=in_ap, **kw),
                       ins=ins, outs=outs, sem=sem, amt=16, extra=extra)

    def current(self, sem):
        return (sem, self.cnt[sem])

    def fence(self, include_cc=False):
        toks = [(sm, self.cnt[sm]) for sm in ("pe", "dve", "act", "pool")]
        for sm in self.DMA_SEMS:
            if sm == "cc" and not include_cc:
                continue
            toks.append(self.publish(sm, self.cnt[sm]))
        for eng in self.ENGINES:
            for (sm, v) in toks:
                self._wait(eng, sm, v)


def ceil_div(a, b):
    return -(-a // b)


def build_program(cfg, debug=False):
    B, L, DM, DI = cfg["B"], cfg["L"], cfg["DM"], cfg["DI"]
    DS, DTR, HFF, LC, PSF = cfg["DS"], cfg["DTR"], cfg["HFF"], cfg["LC"], cfg["PSF"]
    TOK = B * L
    TOKC = TOK // NC_
    DCORE = DI // NC_
    KC = DM // 128
    NDC = DCORE // 128
    HC = HFF // 128
    NSLOT = ceil_div(HC, NC_)
    KDI = DI // 128
    NCH = TOK // LC
    NF = LC // PSF
    MDM = DM // 128
    TTOK = TOKC // 128
    NXP = DTR + 2 * DS
    assert LC % PSF == 0 and TOKC % LC == 0 and L % LC == 0 and DTR == 128

    nc = bass.Bass("TRN2", target_bir_lowering=False, debug=False)

    def din(name, shape, dt=F32):
        return nc.dram_tensor(name, shape, dt, kind="ExternalInput").ap()

    def dout(name, shape, dt=F32):
        return nc.dram_tensor(name, shape, dt, kind="ExternalOutput").ap()

    def dint(name, shape, dt):
        return nc.dram_tensor(name, shape, dt, kind="Internal").ap()

    x_tok = din("x_tok", [TOKC, DM])
    winT_x = din("winT_x", [DM, DCORE])
    winT_z = din("winT_z", [DM, DCORE])
    convw = din("convw", [DCORE, 4])
    convb = din("convb", [DCORE, 1])
    negA_in = din("negA_in", [DCORE, DS])
    D_in = din("D_in", [DCORE, 1])
    dtb_in = din("dtb_in", [DCORE, 1])
    dtwT = din("dtwT", [DTR, DCORE])
    xpwT = din("xpwT", [DCORE, NXP])
    woT_sl = din("woT_sl", [DCORE, DM])
    w1T_sl = din("w1T_sl", [DM, NSLOT * 128])
    w2T_sl = din("w2T_sl", [DM, NSLOT * 128])
    w3T_sl = din("w3T_sl", [NSLOT * 128, DM])
    ones_bf = din("ones_bf", [1, 128], BF16)
    ones_fr = din("ones_fr", [1, 128])
    ones_fc = din("ones_fc", [128, 1])
    eye_bf = din("eye_bf", [128, 128], BF16)
    sel_in = din("sel_in", [2 * DS, 2 * DS * 128], BF16)
    eye_f = din("eye_f", [128, 128])
    out_ext = dout("out_sl", [TOKC, DM])

    NQX = max(1, TOKC // max(LC, PSF * 2, TOKC // 4))
    QTOK = TOKC // NQX
    xn_agi = dint("xn_agi", [NQX * DM, QTOK], BF16)
    xn_ago = dint("xn_ago", [NC_ * NQX * DM, QTOK], BF16)
    gp_ari = dint("gp_ari", [1, 4], F32)
    gp_aro = dint("gp_aro", [1, 4], F32)
    NDD = DCORE // 128
    L1R = NSLOT * KC * 128
    L3R = MDM * NSLOT * 128
    LOR = MDM * NDD * 128
    w1_agi = dint("w1_agi", [128, L1R], BF16)
    w1_ago = dint("w1_ago", [NC_ * 128, L1R], BF16)
    w2_agi = dint("w2_agi", [128, L1R], BF16)
    w2_ago = dint("w2_ago", [NC_ * 128, L1R], BF16)
    w3_agi = dint("w3_agi", [128, L3R], BF16)
    w3_ago = dint("w3_ago", [NC_ * 128, L3R], BF16)
    wo_agi = dint("wo_agi", [128, LOR], BF16)
    wo_ago = dint("wo_ago", [NC_ * 128, LOR], BF16)
    LW = 2 * NDC * KC * 128
    win_img = dint("win_img", [128, LW], BF16)
    xdbl_p = dint("xdbl_p", [NCH * NXP, LC], BF16)
    xdbl_f = dint("xdbl_f", [NCH * NXP, LC], BF16)
    hid_dram = dint("hid_dram", [HFF, TOKC], BF16)
    x3_dram = dint("x3_dram", [DM, TOKC], F32)
    yg_a2ai = dint("yg_a2ai", [NC_ * DCORE, TOKC], BF16)
    yg_a2ao = dint("yg_a2ao", [NC_ * DCORE, TOKC], BF16)

    if debug:
        dbg = {}
        for nm, src, shape, dt in [
                ("dbg_xn", xn_agi, [NQX * DM, QTOK], BF16),
                ("dbg_xdbl", xdbl_f, [NCH * NXP, LC], BF16),
                ("dbg_yg", yg_a2ai, [NC_ * DCORE, TOKC], BF16),
                ("dbg_hid", hid_dram, [HFF, TOKC], BF16),
                ("dbg_x3", x3_dram, [DM, TOKC], F32),
                ("dbg_w1q", w1_ago, [NC_ * 128, NSLOT * KC * 128], BF16)]:
            dbg[nm] = (dout(nm, shape, dt), src)

    rg = [list(range(NC_))]
    ar_tok = {}

    with contextlib.ExitStack() as stack:
        S = Sched(nc, stack)

        def sb(name, shape, dt):
            return stack.enter_context(nc.sbuf_tensor(name, shape, dt))

        def mk_psum(st):
            def f(name, shape, dt=F32):
                return st.enter_context(nc.psum_tensor(name, shape, dt))
            return f

        eyeb_t = sb("eyeb_t", [128, 128], BF16)
        sel_t = sb("sel_t", [2 * DS, 2 * DS * 128], BF16)
        eyef_t = sb("eyef_t", [128, 128], F32)
        onesb_t = sb("onesb_t", [1, 128], BF16)
        onesfr_t = sb("onesfr_t", [1, 128], F32)
        onesfc_t = sb("onesfc_t", [128, 1], F32)
        negA_t = sb("negA_t", [128, NDC * DS], F32)
        convw_t = sb("convw_t", [128, NDC * 4], F32)
        convb_t = sb("convb_t", [128, NDC], F32)
        D_t = sb("D_t", [128, NDC], F32)
        dtb_t = sb("dtb_t", [128, NDC], F32)
        dtw_t = sb("dtw_t", [128, DCORE], BF16)
        xpw_t = sb("xpw_t", [128, NDC * NXP], BF16)
        state_t = sb("state_t", [128, NDC * DS], F32)
        ginv_t = sb("ginv_t", [128, 4], F32)

        S.dma("sync", eyeb_t[:], eye_bf, outs=["eyeb"])
        S.dma("sync", sel_t[:], sel_in, outs=["selt"])
        S.dma("sync", eyef_t[:], eye_f, outs=["eyef"])
        S.dma("sync", onesb_t[:], ones_bf, outs=["onesb"])
        S.dma("sync", onesfr_t[:], ones_fr, outs=["onesfr"])
        S.dma("sync", onesfc_t[:], ones_fc, outs=["onesfc"])
        for dc in range(NDC):
            S.dma("sync", negA_t[:, dc * DS:(dc + 1) * DS],
                  negA_in[dc * 128:(dc + 1) * 128, :], outs=["negA"])
            S.dma("sync", convw_t[:, dc * 4:(dc + 1) * 4],
                  convw[dc * 128:(dc + 1) * 128, :], outs=["convw"])
            S.dma("sync", convb_t[:, dc:dc + 1],
                  convb[dc * 128:(dc + 1) * 128, :], outs=["convb"])
            S.dma("sync", D_t[:, dc:dc + 1],
                  D_in[dc * 128:(dc + 1) * 128, :], outs=["Dt"])
            S.dma("sync", dtb_t[:, dc:dc + 1],
                  dtb_in[dc * 128:(dc + 1) * 128, :], outs=["dtb"])

        with contextlib.ExitStack() as ph:
            def sb0(name, shape, dt):
                return ph.enter_context(nc.sbuf_tensor(name, shape, dt))
            psum_ = mk_psum(ph)
            xt = [sb0(f"p0_xt{i}", [128, DM], F32) for i in range(4)]
            xnb = [sb0(f"p0_xnb{i}", [128, DM], BF16) for i in range(4)]
            red2 = [sb0(f"p0_red{i}", [128, 8], F32) for i in range(2)]
            sq2 = [sb0(f"p0_sq{i}", [128, DM], F32) for i in range(2)]
            evb = [sb0(f"p0_evb{i}", [128, PSF], BF16) for i in range(2)]
            ps_tr = [psum_(f"p0_pstr{i}", [128, PSF], BF16) for i in range(2)]
            WMAX = max(DM, NSLOT * 128)
            wt2 = [sb0(f"p0_wt{i}", [128, WMAX], F32) for i in range(4)]
            WSEM = ["dld", "dgp", "dld", "dgp"]
            wq_bf = sb0("p0_wqb", [128, WMAX], BF16)
            gpos = sb0("p0_gpos", [128, WMAX], F32)
            TRP = PSF // 128

            def xload(t):
                b = t % 4
                S.dma("sync", xt[b][:], x_tok[t * 128:(t + 1) * 128, :],
                      outs=[f"p0x{b}"], sem=("dld" if b % 2 == 0 else "dgp"))
            for t in range(min(3, TTOK)):
                xload(t)
            for t in range(TTOK):
                b = t % 4
                pr = t % 2
                red = red2[pr]
                sq = sq2[pr]
                rk = f"p0red{pr}"
                xk = f"p0x{b}"
                if t + 3 < TTOK:
                    xload(t + 3)
                S.op("vector", lambda e, b=b, red=red: e.tensor_reduce(
                    out=red[:, 0:1], in_=xt[b][:], axis=AX.X, op=ALU.add),
                    ins=[xk], outs=[rk])
                S.op("scalar", lambda e, b=b, red=red, sq=sq: e.activation(
                    out=sq[:], in_=xt[b][:], func=AF.Square, accum_out=red[:, 1:2]),
                    ins=[xk], outs=[f"p0sq{pr}", rk])
                S.op("vector", lambda e, red=red: e.tensor_scalar(
                    out=red[:, 2:3], in0=red[:, 0:1], scalar1=1.0 / DM,
                    scalar2=None, op0=ALU.mult), ins=[rk], outs=[rk])
                S.op("vector", lambda e, red=red: e.tensor_tensor(
                    out=red[:, 3:4], in0=red[:, 2:3], in1=red[:, 2:3], op=ALU.mult),
                    ins=[rk], outs=[rk])
                S.op("vector", lambda e, red=red: e.tensor_scalar(
                    out=red[:, 4:5], in0=red[:, 1:2], scalar1=1.0 / DM,
                    scalar2=None, op0=ALU.mult), ins=[rk], outs=[rk])
                S.op("vector", lambda e, red=red: e.tensor_tensor(
                    out=red[:, 5:6], in0=red[:, 4:5], in1=red[:, 3:4],
                    op=ALU.subtract), ins=[rk], outs=[rk])
                S.op("vector", lambda e, red=red: e.tensor_scalar(
                    out=red[:, 5:6], in0=red[:, 5:6], scalar1=1e-5, scalar2=None,
                    op0=ALU.add), ins=[rk], outs=[rk])
                S.op("vector", lambda e, red=red: e.reciprocal(
                    out=red[:, 6:7], in_=red[:, 5:6]), ins=[rk], outs=[rk])
                S.op("scalar", lambda e, red=red: e.activation(
                    out=red[:, 7:8], in_=red[:, 6:7], func=AF.Sqrt),
                    ins=[rk], outs=[rk])
                xnk = f"p0xn{t % 4}"
                S.op("vector", lambda e, b=b, t=t, red=red: e.tensor_scalar(
                    out=xnb[t % 4][:], in0=xt[b][:], scalar1=red[:, 2:3],
                    scalar2=red[:, 7:8], op0=ALU.subtract, op1=ALU.mult),
                    ins=[xk, rk], outs=[xnk])
                if t % TRP == TRP - 1:
                    g = t // TRP
                    pb = g % 2
                    pk = f"p0ps{pb}"
                    ek = f"p0ev{pb}"
                    for d_c in range(KC):
                        for j in range(TRP):
                            tt = g * TRP + j
                            S.op("tensor", lambda e, d_c=d_c, j=j, tt=tt, pb=pb:
                                 e.transpose(
                                     out=ps_tr[pb][:, j * 128:(j + 1) * 128],
                                     in_=xnb[tt % 4][:, d_c * 128:(d_c + 1) * 128],
                                     identity=eyeb_t[:]),
                                 ins=[f"p0xn{tt % 4}", "eyeb"], outs=[pk])
                        S.op("scalar", lambda e, pb=pb: e.activation(
                            out=evb[pb][:], in_=ps_tr[pb][:], func=AF.Identity),
                            ins=[pk], outs=[ek])
                        qq = (g * PSF) // QTOK
                        qc = (g * PSF) % QTOK
                        S.dma("sync", xn_agi[qq * DM + d_c * 128:
                                             qq * DM + (d_c + 1) * 128,
                                             qc:qc + PSF],
                              evb[pb][:], ins=[ek], outs=[f"dram_xnagi{qq}"],
                              sem="dst")

            for qq in range(NQX):
                S.op("gpsimd", lambda e, qq=qq: e.collective_compute(
                    "AllGather", ALU.bypass, replica_groups=rg,
                    ins=[xn_agi[qq * DM:(qq + 1) * DM, :].opt()],
                    outs=[xn_ago[qq * NC_ * DM:(qq + 1) * NC_ * DM, :].opt()]),
                    ins=[f"dram_xnagi{qq}"], outs=[f"dram_xnago{qq}"], sem="cc",
                    extra=[S.current("dst")])
            for half, wsrc in ((0, winT_x), (1, winT_z)):
                for kc in range(KC):
                    wb_ = kc % 2
                    S.dma("sync", wt2[wb_][:, 0:DCORE],
                          wsrc[kc * 128:(kc + 1) * 128, :],
                          outs=[f"p0wt{wb_}"], sem=("dld" if wb_ == 0 else "dx"))
                    S.op("vector", lambda e, wb_=wb_: e.tensor_copy(
                        out=wq_bf[:, 0:DCORE], in_=wt2[wb_][:, 0:DCORE]),
                        ins=[f"p0wt{wb_}"], outs=["p0wqb"])
                    out_ap = bass.AP(win_img.tensor,
                                     (half * NDC * KC + kc) * 128,
                                     [[LW, 128], [KC * 128, NDC], [1, 128]])
                    S.dma("sync", out_ap, wq_bf[:, 0:DCORE],
                          ins=["p0wqb"], outs=["dram_winimg"], sem="dst")
            S.dma("sync", wt2[0][:, 0:DCORE], dtwT, outs=["p0wt0"])
            S.op("vector", lambda e: e.tensor_copy(
                out=dtw_t[:], in_=wt2[0][:, 0:DCORE]), ins=["p0wt0"], outs=["dtw"])
            for dc in range(NDC):
                wb_ = dc % 2
                S.dma("sync", wt2[wb_][:, 0:NXP], xpwT[dc * 128:(dc + 1) * 128, :],
                      outs=[f"p0wt{wb_}"], sem=("dld" if wb_ == 0 else "dx"))
                S.op("vector", lambda e, dc=dc, wb_=wb_: e.tensor_copy(
                    out=xpw_t[:, dc * NXP:(dc + 1) * NXP],
                    in_=wt2[wb_][:, 0:NXP]),
                    ins=[f"p0wt{wb_}"], outs=["xpw"])

            wabs = sb0("p0_wabs", [128, WMAX], F32)
            acc3 = sb0("p0_acc3", [128, 4], F32)
            accr = sb0("p0_accr", [128, 1], F32)
            ps_g = psum_("p0_psg", [4, 1], F32)
            gp_sb = sb0("p0_gp", [1, 4], F32)
            gpc_sb = sb0("p0_gpc", [4, 1], F32)
            ps_gb = psum_("p0_psgb", [128, 4], F32)
            S.op("vector", lambda e: e.memset(acc3[:], 0.0), outs=["p0acc3"])
            wsl = [(w1T_sl, DM, NSLOT * 128, 0, w1_agi),
                   (w2T_sl, DM, NSLOT * 128, 1, w2_agi),
                   (w3T_sl, NSLOT * 128, DM, 2, w3_agi)]
            wtiles = []
            for (wsrc, rows, cols, j, _agi) in wsl:
                for r in range(rows // 128):
                    wtiles.append((wsrc, r, cols, j))

            def wload(i):
                wsrc, r, cols, j = wtiles[i]
                wb_ = i % 4
                S.dma("sync", wt2[wb_][:, 0:cols],
                      wsrc[r * 128:(r + 1) * 128, :],
                      outs=[f"p0wt{wb_}"], sem=WSEM[wb_])
            for i in range(min(3, len(wtiles))):
                wload(i)
            for i in range(len(wtiles)):
                wsrc, r, cols, j = wtiles[i]
                wb_ = i % 4
                S.op("scalar", lambda e, cols=cols, wb_=wb_: e.activation(
                    out=wabs[:, 0:cols], in_=wt2[wb_][:, 0:cols], func=AF.Abs,
                    accum_out=accr[:]), ins=[f"p0wt{wb_}"],
                    outs=["p0wabs", "p0accr"])
                S.op("vector", lambda e, j=j: e.tensor_tensor(
                    out=acc3[:, j:j + 1], in0=acc3[:, j:j + 1], in1=accr[:],
                    op=ALU.add), ins=["p0accr"], outs=["p0acc3"])
                if i + 3 < len(wtiles):
                    wload(i + 3)
            S.op("tensor", lambda e: e.matmul(
                out=ps_g[:], lhsT=acc3[:, 0:4], rhs=onesfc_t[:], start=True,
                stop=True), ins=["p0acc3", "onesfc"], outs=["p0psg"])
            S.op("vector", lambda e: e.tensor_copy(out=gpc_sb[:], in_=ps_g[:]),
                 ins=["p0psg"], outs=["p0gpc"])
            S.dma("sync", gp_ari.rearrange("a b -> b a"), gpc_sb[:],
                  ins=["p0gpc"], outs=["dram_gpi"], sem="dst")
            S.op("gpsimd", lambda e: e.collective_compute(
                "AllReduce", ALU.add, replica_groups=rg,
                ins=[gp_ari.opt()], outs=[gp_aro.opt()]),
                ins=["dram_gpi"], outs=["dram_gpo"], sem="cc",
                extra=[S.current("dst")])
            NW = float(DM * HFF)
            S.dma("sync", gp_sb[:], gp_aro, ins=["dram_gpo"], outs=["p0gp"])
            S.op("vector", lambda e: e.tensor_scalar(
                out=gp_sb[:], in0=gp_sb[:], scalar1=1.0 / NW, scalar2=1e-8,
                op0=ALU.mult, op1=ALU.add), ins=["p0gp"], outs=["p0gp"])
            S.op("vector", lambda e: e.reciprocal(out=gp_sb[:], in_=gp_sb[:]),
                 ins=["p0gp"], outs=["p0gp"])
            S.op("tensor", lambda e: e.matmul(
                out=ps_gb[:], lhsT=onesfr_t[:], rhs=gp_sb[:], start=True,
                stop=True), ins=["p0gp", "onesfr"], outs=["p0psgb"])
            S.op("vector", lambda e: e.tensor_copy(out=ginv_t[:], in_=ps_gb[:]),
                 ins=["p0psgb"], outs=["ginv"])

            qtiles = []
            for (wsrc, rows, cols, j, agi) in wsl:
                for r in range(rows // 128):
                    qtiles.append((wsrc, r, cols, j, agi))

            def qload(i):
                wsrc, r, cols, j, agi = qtiles[i]
                wb_ = i % 4
                S.dma("sync", wt2[wb_][:, 0:cols],
                      wsrc[r * 128:(r + 1) * 128, :],
                      outs=[f"p0wt{wb_}"], sem=WSEM[wb_])
            for i in range(min(3, len(qtiles))):
                qload(i)
            for i in range(len(qtiles)):
                wsrc, r, cols, j, agi = qtiles[i]
                wb_ = i % 4
                if True:
                    S.op("vector", lambda e, cols=cols, j=j, wb_=wb_:
                         e.tensor_scalar(
                             out=wabs[:, 0:cols], in0=wt2[wb_][:, 0:cols],
                             scalar1=ginv_t[:, j:j + 1], scalar2=None,
                             op0=ALU.mult),
                        ins=[f"p0wt{wb_}", "ginv"], outs=["p0wabs"])
                    S.op("vector", lambda e, cols=cols: e.tensor_scalar(
                        out=gpos[:, 0:cols], in0=wabs[:, 0:cols], scalar1=0.5,
                        scalar2=None, op0=ALU.is_gt), ins=["p0wabs"],
                        outs=["p0gpos"])
                    S.op("vector", lambda e, cols=cols: e.tensor_scalar(
                        out=wabs[:, 0:cols], in0=wabs[:, 0:cols], scalar1=-0.5,
                        scalar2=None, op0=ALU.is_lt), ins=["p0wabs"],
                        outs=["p0wabs"])
                    S.op("vector", lambda e, cols=cols: e.tensor_tensor(
                        out=wq_bf[:, 0:cols], in0=gpos[:, 0:cols],
                        in1=wabs[:, 0:cols], op=ALU.subtract),
                        ins=["p0gpos", "p0wabs"], outs=["p0wqb"])
                    if j < 2:
                        out_ap = bass.AP(agi.tensor, r * 128,
                                         [[L1R, 128], [KC * 128, NSLOT], [1, 128]])
                        S.dma("sync", out_ap, wq_bf[:, 0:NSLOT * 128],
                              ins=["p0wqb"], outs=[f"dram_wagi{j}"], sem="dst")
                    else:
                        out_ap = bass.AP(agi.tensor, r * 128,
                                         [[L3R, 128], [NSLOT * 128, MDM], [1, 128]])
                        S.dma("sync", out_ap, wq_bf[:, 0:DM],
                              ins=["p0wqb"], outs=[f"dram_wagi{j}"], sem="dst")
                    if i + 3 < len(qtiles):
                        qload(i + 3)
            for j, (agi, ago) in enumerate([(w1_agi, w1_ago), (w2_agi, w2_ago),
                                            (w3_agi, w3_ago)]):
                S.op("gpsimd", lambda e, agi=agi, ago=ago: e.collective_compute(
                    "AllGather", ALU.bypass, replica_groups=rg,
                    ins=[agi.opt()], outs=[ago.opt()]),
                    ins=[f"dram_wagi{j}"], outs=[f"dram_wago{j}"], sem="cc",
                    extra=[S.current("dst")])

            for r in range(DCORE // 128):
                wb_ = r % 2
                S.dma("sync", wt2[wb_][:, 0:DM], woT_sl[r * 128:(r + 1) * 128, :],
                      outs=[f"p0wt{wb_}"], sem=("dld" if wb_ == 0 else "dx"))
                S.op("vector", lambda e, wb_=wb_: e.tensor_copy(
                    out=wq_bf[:, 0:DM], in_=wt2[wb_][:, 0:DM]),
                    ins=[f"p0wt{wb_}"], outs=["p0wqb"])
                out_ap = bass.AP(wo_agi.tensor, r * 128,
                                 [[LOR, 128], [NDD * 128, MDM], [1, 128]])
                S.dma("sync", out_ap, wq_bf[:, 0:DM],
                      ins=["p0wqb"], outs=["dram_woagi"], sem="dst")
            S.op("gpsimd", lambda e: e.collective_compute(
                "AllGather", ALU.bypass, replica_groups=rg,
                ins=[wo_agi.opt()], outs=[wo_ago.opt()]),
                ins=["dram_woagi"], outs=["dram_woago"], sem="cc",
                extra=[S.current("dst")])

            S.op("vector", lambda e: e.memset(state_t[:], 0.0), outs=["state"])

        S.fence()
        with contextlib.ExitStack() as ph:
            def sbM(name, shape, dt):
                return ph.enter_context(nc.sbuf_tensor(name, shape, dt))
            psum_ = mk_psum(ph)
            HLC = LC // 2
            win2 = [sbM(f"m_win{i}", [128, KC * 128], BF16) for i in range(2)]
            xnm = [[sbM(f"m_xn{h}_{k}", [128, HLC], BF16) for k in range(KC)]
                   for h in range(2)]
            xp_ = [sbM(f"m_xp{dc}", [128, LC + 3], BF16) for dc in range(NDC)]
            hsv = sbM("m_hsv", [128, 4 * NDC], BF16)
            cv1 = sbM("m_cv1", [128, LC], BF16)
            cv2 = sbM("m_cv2", [128, LC], BF16)
            u_b = [[sbM(f"m_u{dc}_{i}", [128, LC], BF16) for i in range(2)]
                   for dc in range(NDC)]
            sz_b = [[sbM(f"m_sz{dc}_{i}", [128, LC], BF16) for i in range(2)]
                    for dc in range(NDC)]
            xdt_b = sbM("m_xdt", [128, LC], BF16)
            xbc_b = sbM("m_xbc", [32, LC], BF16)
            dt_bf = [sbM(f"d_dtbf{i}", [128, LC], BF16) for i in range(2)]
            bc32b = [sbM(f"d_bc32b{i}", [32, LC], BF16) for i in range(2)]
            bbc = [sbM(f"d_bbc{i}", [128, LC], BF16) for i in range(2)]
            cbc = [sbM(f"d_cbc{i}", [128, LC], BF16) for i in range(2)]
            delta = [[sbM(f"d_delta{dc}_{i}", [128, LC], BF16) for i in range(2)]
                     for dc in range(NDC)]
            du_b = [[sbM(f"d_du{dc}_{i}", [128, LC], BF16) for i in range(2)]
                    for dc in range(NDC)]
            yacc = [sbM(f"d_yacc{dc}", [128, LC], BF16) for dc in range(NDC)]
            sc1 = sbM("d_sc1", [128, PSF], BF16)
            dA_ = [sbM(f"d_dA{i}", [128, LC], BF16) for i in range(2)]
            dBu_ = [sbM(f"d_dBu{i}", [128, LC], BF16) for i in range(2)]
            so_ = [sbM(f"d_so{i}", [128, LC], BF16) for i in range(2)]
            FHT = max(TOKC // 8, PSF)
            NFH = TOKC // FHT
            NFF = FHT // PSF
            xnf = [sbM(f"f_xn{k}", [128, FHT], BF16) for k in range(KC)]
            w12 = [sbM(f"f_w{i}", [128, 2 * KC * 128], BF16) for i in range(2)]
            tb_ = [sbM(f"f_tb{i}", [128, PSF], BF16) for i in range(2)]
            x2h = [sbM(f"f_x2h{i}", [128, PSF], BF16) for i in range(2)]
            hb_ = [sbM(f"f_hb{i}", [128, PSF], BF16) for i in range(2)]
            psA = psum_("m_psA", [128, PSF], F32)
            psB = psum_("m_psB", [128, PSF], F32)
            psC = psum_("m_psC", [128, PSF], F32)
            psD = psum_("m_psD", [128, PSF], F32)
            psE = psum_("m_psE", [128, PSF], F32)
            psF1 = psum_("m_psF1", [128, PSF], F32)
            psF2 = psum_("m_psF2", [128, PSF], F32)
            psF3 = psum_("m_psF3", [128, PSF], F32)

            def emit_p1_start(c):
                g0 = c * LC
                r0 = g0 // TOKC
                col0 = g0 % TOKC
                qq = col0 // QTOK
                qc = col0 % QTOK
                base = (qq * NC_ + r0) * DM
                for h in range(2):
                    for k in range(KC):
                        S.dma("sync", xnm[h][k][:],
                              xn_ago[base + k * 128:base + (k + 1) * 128,
                                     qc + h * HLC:qc + (h + 1) * HLC],
                              ins=[f"dram_xnago{qq}"], outs=[f"p1xn{h}_{k}"],
                              sem="dx")

            def emit_p1_m(c, m):
                cb_ = c % 2
                half = m // NDC
                dc = m % NDC
                wmb = m % 2
                wmk = f"p1w{wmb}"
                S.dma("sync", win2[wmb][:],
                      win_img[:, m * KC * 128:(m + 1) * KC * 128],
                      ins=["dram_winimg"], outs=[wmk], sem="dw")
                for h in range(2):
                    pb = (m * 2 + h) % 2
                    psx = psA if pb == 0 else psB
                    pk = f"p1ps{pb}"
                    S.op("tensor", lambda e, m=m, h=h, wmb=wmb, psx=psx: [
                        e.matmul(out=psx[:, 0:HLC],
                                 lhsT=win2[wmb][:, k * 128:(k + 1) * 128],
                                 rhs=xnm[h][k][:],
                                 start=(k == 0), stop=(k == KC - 1))
                        for k in range(KC)][-1],
                        ins=[wmk] + [f"p1xn{h}_{k}" for k in range(KC)],
                        outs=[pk], sem="pe")
                    if half == 0:
                        colf = 3 + h * HLC
                        S.op("scalar", lambda e, dc=dc, psx=psx, colf=colf:
                             e.activation(
                                 out=xp_[dc][:, colf:colf + HLC],
                                 in_=psx[:, 0:HLC], func=AF.Identity),
                             ins=[pk], outs=[f"p1xp{dc}"])
                    else:
                        S.op("scalar", lambda e, dc=dc, h=h, psx=psx, cb_=cb_:
                             e.activation(
                                 out=sz_b[dc][cb_][:, h * HLC:(h + 1) * HLC],
                                 in_=psx[:, 0:HLC], func=AF.Silu),
                             ins=[pk], outs=[f"p1sz{dc}_{cb_}"])

            def emit_p1_conv(c):
                g0 = c * LC
                cb_ = c % 2
                batch_first = (g0 % L) == 0
                for dc in range(NDC):
                    xpk = f"p1xp{dc}"
                    if batch_first:
                        S.op("vector", lambda e, dc=dc: e.memset(
                            xp_[dc][:, 0:3], 0.0), outs=[xpk])
                    else:
                        S.op("vector", lambda e, dc=dc: e.tensor_copy(
                             out=xp_[dc][:, 0:3],
                             in_=hsv[:, dc * 4:dc * 4 + 3]),
                             ins=["p1hsv"], outs=[xpk])
                    for tap in range(4):
                        dstb = cv1 if tap == 0 else cv2
                        dk = "p1cv1" if tap == 0 else "p1cv2"
                        S.op("scalar", lambda e, dc=dc, tap=tap,
                             d=dstb: e.activation(
                                 out=d[:], in_=xp_[dc][:, tap:tap + LC],
                                 func=AF.Identity,
                                 scale=convw_t[:, dc * 4 + tap:
                                               dc * 4 + tap + 1]),
                            ins=[xpk, "convw"], outs=[dk])
                        if tap > 0:
                            S.op("vector", lambda e: e.tensor_tensor(
                                out=cv1[:], in0=cv1[:], in1=cv2[:], op=ALU.add),
                                ins=["p1cv1", "p1cv2"], outs=["p1cv1"])
                    S.op("vector", lambda e, dc=dc: e.tensor_copy(
                        out=hsv[:, dc * 4:dc * 4 + 3],
                        in_=xp_[dc][:, LC:LC + 3]), ins=[xpk], outs=["p1hsv"])
                    S.op("scalar", lambda e, dc=dc, cb_=cb_: e.activation(
                        out=u_b[dc][cb_][:], in_=cv1[:], func=AF.Silu,
                        bias=convb_t[:, dc:dc + 1]),
                        ins=["p1cv1", "convb"], outs=[f"p1u{dc}_{cb_}"])
                for f in range(NF):
                    for dc in range(NDC):
                        S.op("tensor", lambda e, dc=dc, f=f, cb_=cb_: e.matmul(
                            out=psA[:],
                            lhsT=xpw_t[:, dc * NXP:dc * NXP + 128],
                            rhs=u_b[dc][cb_][:, f * PSF:(f + 1) * PSF],
                            start=(dc == 0), stop=(dc == NDC - 1)),
                            ins=["xpw", f"p1u{dc}_{cb_}"], outs=["p1ps0"])
                        S.op("tensor", lambda e, dc=dc, f=f, cb_=cb_: e.matmul(
                            out=psB[0:32, :],
                            lhsT=xpw_t[:, dc * NXP + 128:dc * NXP + NXP],
                            rhs=u_b[dc][cb_][:, f * PSF:(f + 1) * PSF],
                            start=(dc == 0), stop=(dc == NDC - 1)),
                            ins=["xpw", f"p1u{dc}_{cb_}"], outs=["p1ps1"])
                    S.op("scalar", lambda e, f=f: e.activation(
                        out=xdt_b[:, f * PSF:(f + 1) * PSF], in_=psA[:],
                        func=AF.Identity), ins=["p1ps0"], outs=["p1xdt"])
                    S.op("scalar", lambda e, f=f: e.activation(
                        out=xbc_b[:, f * PSF:(f + 1) * PSF], in_=psB[0:32, :],
                        func=AF.Identity), ins=["p1ps1"], outs=["p1xbc"])
                S.dma("sync", xdbl_p[c * NXP:c * NXP + 128, :], xdt_b[:],
                      ins=["p1xdt"], outs=[f"dram_xdp_{c}"], sem="dst")
                S.dma("sync", xdbl_p[c * NXP + 128:(c + 1) * NXP, :], xbc_b[:],
                      ins=["p1xbc"], outs=[f"dram_xdp_{c}"], sem="dst")
                ar_tok[c] = S.op("gpsimd", lambda e, c=c: e.collective_compute(
                    "AllReduce", ALU.add, replica_groups=rg,
                    ins=[xdbl_p[c * NXP:(c + 1) * NXP, :].opt()],
                    outs=[xdbl_f[c * NXP:(c + 1) * NXP, :].opt()]),
                    ins=[f"dram_xdp_{c}"], outs=[f"dram_xdf_{c}"], sem="cc",
                    extra=[S.current("dst")])

            def emit_d_load(c):
                cb_ = c % 2
                S.dma("sync", dt_bf[cb_][:], xdbl_f[c * NXP:c * NXP + 128, :],
                      ins=[f"dram_xdf_{c}"], outs=[f"d_dtbf{cb_}"], sem="dx")
                S.dma("sync", bc32b[cb_][:],
                      xdbl_f[c * NXP + 128:(c + 1) * NXP, :],
                      ins=[f"dram_xdf_{c}"], outs=[f"d_bc32b{cb_}"], sem="dx")

            def emit_d_head(c):
                g0 = c * LC
                cb_ = c % 2
                for dc in range(NDC):
                    for f in range(NF):
                        S.op("tensor", lambda e, dc=dc, f=f, cb_=cb_: e.matmul(
                            out=psC[:],
                            lhsT=dtw_t[:, dc * 128:(dc + 1) * 128],
                            rhs=dt_bf[cb_][:, f * PSF:(f + 1) * PSF],
                            start=True, stop=True),
                            ins=["dtw", f"d_dtbf{cb_}"], outs=["d_pC"])
                        S.op("scalar", lambda e, dc=dc, f=f: e.activation(
                            out=sc1[:], in_=psC[:], func=AF.Exp,
                            bias=dtb_t[:, dc:dc + 1]),
                            ins=["d_pC", "dtb"], outs=["d_sc1"])
                        S.op("scalar", lambda e, dc=dc, f=f, cb_=cb_:
                             e.activation(
                                 out=delta[dc][cb_][:, f * PSF:(f + 1) * PSF],
                                 in_=sc1[:], func=AF.Ln, bias=1.0),
                            ins=["d_sc1"], outs=[f"d_del{dc}_{cb_}"])
                    S.op("vector", lambda e, dc=dc, cb_=cb_: e.tensor_tensor(
                        out=du_b[dc][cb_][:], in0=delta[dc][cb_][:],
                        in1=u_b[dc][cb_][:],
                        op=ALU.mult), ins=[f"d_del{dc}_{cb_}", f"p1u{dc}_{cb_}"],
                        outs=[f"d_du{dc}_{cb_}"])
                if (g0 % L) == 0:
                    S.op("vector", lambda e: e.memset(state_t[:], 0.0),
                         outs=["state"])

            def emit_d_scan(c, inject=None):
                g0 = c * LC
                cb_ = c % 2
                for n in range(DS):
                    if inject:
                        inject(n)
                    nb = n % 2
                    bbk, cbk = f"d_bbc{nb}", f"d_cbc{nb}"
                    for f in range(NF):
                        S.op("tensor", lambda e, n=n, f=f, cb_=cb_: e.matmul(
                            out=psD[:], lhsT=sel_t[:, n * 128:(n + 1) * 128],
                            rhs=bc32b[cb_][:, f * PSF:(f + 1) * PSF],
                            start=True, stop=True),
                            ins=[f"d_bc32b{cb_}", "selt"], outs=["d_pD"])
                        S.op("scalar", lambda e, nb=nb, f=f: e.activation(
                            out=bbc[nb][:, f * PSF:(f + 1) * PSF],
                            in_=psD[:], func=AF.Identity),
                            ins=["d_pD"], outs=[bbk])
                        S.op("tensor", lambda e, n=n, f=f, cb_=cb_: e.matmul(
                            out=psE[:],
                            lhsT=sel_t[:, (DS + n) * 128:(DS + n + 1) * 128],
                            rhs=bc32b[cb_][:, f * PSF:(f + 1) * PSF],
                            start=True, stop=True),
                            ins=[f"d_bc32b{cb_}", "selt"], outs=["d_pE"])
                        S.op("scalar", lambda e, nb=nb, f=f: e.activation(
                            out=cbc[nb][:, f * PSF:(f + 1) * PSF],
                            in_=psE[:], func=AF.Identity),
                            ins=["d_pE"], outs=[cbk])
                    for dc in range(NDC):
                        tb = (n * NDC + dc) % 2
                        ak, uk2 = f"d_dA{tb}", f"d_dBu{tb}"
                        sok, cek = f"d_so{tb}", f"d_ce{tb}"
                        S.op("scalar", lambda e, dc=dc, n=n, tb=tb, cb_=cb_:
                             e.activation(
                                 out=dA_[tb][:], in_=delta[dc][cb_][:],
                                 func=AF.Exp,
                                 scale=negA_t[:, dc * DS + n:dc * DS + n + 1]),
                            ins=[f"d_del{dc}_{cb_}", "negA"], outs=[ak])
                        S.op("vector", lambda e, dc=dc, tb=tb, nb=nb, cb_=cb_:
                             e.tensor_tensor(
                                 out=dBu_[tb][:], in0=du_b[dc][cb_][:],
                                 in1=bbc[nb][:], op=ALU.mult),
                             ins=[f"d_du{dc}_{cb_}", bbk], outs=[uk2])
                        S.op("vector", lambda e, dc=dc, n=n, tb=tb:
                             e.tensor_tensor_scan(
                                 out=so_[tb][:], data0=dA_[tb][:],
                                 data1=dBu_[tb][:],
                                 initial=state_t[:, dc * DS + n:
                                                 dc * DS + n + 1],
                                 op0=ALU.mult, op1=ALU.add),
                             ins=[ak, uk2, "state"], outs=[sok])
                        S.op("vector", lambda e, dc=dc, n=n, tb=tb:
                             e.tensor_copy(
                                 out=state_t[:, dc * DS + n:dc * DS + n + 1],
                                 in_=so_[tb][:, LC - 1:LC]),
                             ins=[sok], outs=["state"])
                        S.op("vector", lambda e, tb=tb, nb=nb: e.tensor_tensor(
                            out=so_[tb][:], in0=so_[tb][:], in1=cbc[nb][:],
                            op=ALU.mult), ins=[sok, cbk], outs=[sok])
                        if n == 0:
                            S.op("vector", lambda e, dc=dc, tb=tb:
                                 e.tensor_copy(
                                     out=yacc[dc][:], in_=so_[tb][:]),
                                 ins=[sok], outs=[f"d_y{dc}"])
                        else:
                            S.op("vector", lambda e, dc=dc, tb=tb:
                                 e.tensor_tensor(
                                     out=yacc[dc][:], in0=yacc[dc][:],
                                     in1=so_[tb][:], op=ALU.add),
                                 ins=[sok], outs=[f"d_y{dc}"])
                blk = g0 // TOKC
                colg = g0 % TOKC
                for dc in range(NDC):
                    ygb = so_[0]
                    S.op("vector", lambda e, dc=dc, cb_=cb_, ygb=ygb:
                         e.tensor_scalar(
                             out=ygb[:], in0=u_b[dc][cb_][:],
                             scalar1=D_t[:, dc:dc + 1], scalar2=None,
                             op0=ALU.mult),
                        ins=[f"p1u{dc}_{cb_}", "Dt"], outs=["d_so0"])
                    S.op("vector", lambda e, dc=dc, ygb=ygb: e.tensor_tensor(
                        out=ygb[:], in0=ygb[:], in1=yacc[dc][:], op=ALU.add),
                        ins=[f"d_y{dc}"], outs=["d_so0"])
                    S.op("vector", lambda e, dc=dc, cb_=cb_, ygb=ygb:
                         e.tensor_tensor(
                             out=ygb[:], in0=ygb[:], in1=sz_b[dc][cb_][:],
                             op=ALU.mult),
                        ins=[f"p1sz{dc}_{cb_}"], outs=["d_so0"])
                    S.dma("sync",
                          yg_a2ai[blk * DCORE + dc * 128:
                                  blk * DCORE + (dc + 1) * 128,
                                  colg:colg + LC],
                          ygb[:], ins=["d_so0"], outs=["dram_ygi"], sem="dst")

            NFU = NFH * HC

            def emit_ffn(u):
                fh = u // HC
                h_c = u % HC
                if h_c == 0:
                    qq = (fh * FHT) // QTOK
                    qc = (fh * FHT) % QTOK
                    for k in range(KC):
                        S.dma("sync", xnf[k][:],
                              xn_agi[qq * DM + k * 128:qq * DM + (k + 1) * 128,
                                     qc:qc + FHT],
                              ins=[f"dram_xnagi{qq}"], outs=[f"fxn{k}"], sem="dx")
                wb = h_c % 2
                wk = f"fw{wb}"
                a_r = h_c % NC_
                s_s = h_c // NC_
                S.dma("sync", w12[wb][:, 0:KC * 128],
                      w1_ago[a_r * 128:(a_r + 1) * 128,
                             s_s * KC * 128:(s_s + 1) * KC * 128],
                      ins=["dram_wago0"], outs=[wk], sem="dw")
                S.dma("sync", w12[wb][:, KC * 128:2 * KC * 128],
                      w2_ago[a_r * 128:(a_r + 1) * 128,
                             s_s * KC * 128:(s_s + 1) * KC * 128],
                      ins=["dram_wago1"], outs=[wk], sem="dw")
                for f in range(NFF):
                    fb = f % 2
                    psx = psF1 if fb == 0 else psF2
                    S.op("tensor", lambda e, wb=wb, f=f, psx=psx: [
                        e.matmul(out=psx[:],
                                 lhsT=w12[wb][:, k * 128:(k + 1) * 128],
                                 rhs=xnf[k][:, f * PSF:(f + 1) * PSF],
                                 start=(k == 0), stop=(k == KC - 1))
                        for k in range(KC)][-1],
                        ins=[wk] + [f"fxn{k}" for k in range(KC)],
                        outs=[f"fps{fb}"], sem="pe")
                    S.op("scalar", lambda e, fb=fb, psx=psx: e.activation(
                        out=tb_[fb][:], in_=psx[:], func=AF.Tanh,
                        scale=0.5), ins=[f"fps{fb}"], outs=[f"ftb{fb}"])
                    S.op("tensor", lambda e, wb=wb, f=f: [
                        e.matmul(out=psF3[:],
                                 lhsT=w12[wb][:, (KC + k) * 128:
                                              (KC + k + 1) * 128],
                                 rhs=xnf[k][:, f * PSF:(f + 1) * PSF],
                                 start=(k == 0), stop=(k == KC - 1))
                        for k in range(KC)][-1],
                        ins=[wk] + [f"fxn{k}" for k in range(KC)],
                        outs=["fps2"], sem="pe")
                    S.op("scalar", lambda e, fb=fb: e.activation(
                        out=x2h[fb][:], in_=psF3[:], func=AF.Identity,
                        scale=0.5), ins=["fps2"], outs=[f"fx2h{fb}"])
                    S.op("gpsimd", lambda e, fb=fb: e.tensor_scalar(
                        out=hb_[fb][:], in0=tb_[fb][:], scalar1=1.0,
                        scalar2=None, op0=ALU.add),
                        ins=[f"ftb{fb}"], outs=[f"fhb{fb}"])
                    S.op("gpsimd", lambda e, fb=fb: e.tensor_tensor(
                        out=hb_[fb][:], in0=hb_[fb][:], in1=x2h[fb][:],
                        op=ALU.mult),
                        ins=[f"fx2h{fb}"], outs=[f"fhb{fb}"])
                    S.dma("sync",
                          hid_dram[h_c * 128:(h_c + 1) * 128,
                                   fh * FHT + f * PSF:fh * FHT + (f + 1) * PSF],
                          hb_[fb][:], ins=[f"fhb{fb}"], outs=["dram_hid"],
                          sem="dst")

            emit_p1_start(0)
            for m in range(2 * NDC):
                emit_p1_m(0, m)
            emit_p1_conv(0)
            emit_d_load(0)
            fu = 0
            SKIP0 = min(3, NCH - 1)
            per = ceil_div(NFU, max(1, NCH - 1 - SKIP0))
            emit_d_head(0)
            for c in range(NCH):
                units = []
                if c + 1 < NCH:
                    emit_p1_start(c + 1)
                    for m in range(2 * NDC):
                        units.append(lambda m=m: emit_p1_m(c + 1, m))
                    units.append(lambda: emit_p1_conv(c + 1))
                    units.append(lambda: emit_d_load(c + 1))
                    units.append(lambda: emit_d_head(c + 1))
                stop = min(NFU, max(0, (c + 1 - SKIP0)) * per)
                while fu < stop:
                    units.append(lambda fu=fu: emit_ffn(fu))
                    fu += 1

                def inject(n, units=units):
                    k0 = len(units) * n // DS
                    k1 = len(units) * (n + 1) // DS
                    for k in range(k0, k1):
                        units[k]()
                emit_d_scan(c, inject)
            while fu < NFU:
                emit_ffn(fu)
                fu += 1

            with allow_ici_all_to_all():
                a2a_tok = S.op("gpsimd", lambda e: e.collective_compute(
                    "AllToAll", ALU.bypass, replica_groups=rg,
                    ins=[yg_a2ai.opt()], outs=[yg_a2ao.opt()]),
                    ins=["dram_ygi"], outs=["dram_ygo"], sem="cc",
                    extra=[S.current("dst")])

        S.fence()
        with contextlib.ExitStack() as ph:
            def sbW(name, shape, dt):
                return ph.enter_context(nc.sbuf_tensor(name, shape, dt))
            psum_ = mk_psum(ph)
            NBK = NC_ * NSLOT
            w3t = [sbW(f"f_w3{i}", [128, NBK * 128], BF16) for i in range(2)]
            x3b = [sbW(f"f_x3{i}", [128, PSF], F32) for i in range(2)]
            ps_f = [psum_(f"w3_ps{i}", [128, PSF], F32) for i in range(2)]

            def blk_of(kh):
                return (kh % NC_) * NSLOT + kh // NC_
            hid_all = sbW("f_hall", [128, HC * PSF], BF16)
            for fq in range(TOKC // PSF):
                hv = bass.AP(hid_dram.tensor, fq * PSF,
                             [[TOKC, 128], [128 * TOKC, HC], [1, PSF]])
                S.dma("sync", hid_all[:], hv, ins=["dram_hid"],
                      outs=["fhall"], sem="dx")
                for m in range(MDM):
                    wb = m % 2
                    wk = f"fw3{wb}"
                    w3v = bass.AP(w3_ago.tensor, m * NSLOT * 128,
                                  [[L3R, 128], [128 * L3R, NC_],
                                   [1, NSLOT * 128]])
                    S.dma("sync", w3t[wb][:, 0:NBK * 128], w3v,
                          ins=["dram_wago2"], outs=[wk], sem="dw")
                    pb = m % 2
                    S.op("tensor", lambda e, wb=wb, pb=pb: [
                        e.matmul(out=ps_f[pb][:],
                                 lhsT=w3t[wb][:, blk_of(kh) * 128:
                                              blk_of(kh) * 128 + 128],
                                 rhs=hid_all[:, kh * PSF:(kh + 1) * PSF],
                                 start=(kh == 0),
                                 stop=(kh == HC - 1)) for kh in range(HC)][-1],
                        ins=[wk, "fhall"],
                        outs=[f"w3ps{pb}"], sem="pe")
                    S.op("scalar", lambda e, pb=pb: e.activation(
                        out=x3b[pb][:], in_=ps_f[pb][:], func=AF.Identity),
                        ins=[f"w3ps{pb}"], outs=[f"fx3{pb}"])
                    S.dma("sync", x3_dram[m * 128:(m + 1) * 128,
                                          fq * PSF:(fq + 1) * PSF],
                          x3b[pb][:], ins=[f"fx3{pb}"], outs=["dram_x3"],
                          sem="dst")

        S.fence()
        with contextlib.ExitStack() as ph:
            def sb7(name, shape, dt):
                return ph.enter_context(nc.sbuf_tensor(name, shape, dt))
            psum_ = mk_psum(ph)
            FQ = 1024 if TOKC % 1024 == 0 else PSF
            NQ = TOKC // FQ
            NPF = FQ // PSF
            ygt_all = sb7("s_ygall", [128, KDI * FQ], BF16)
            wot = [sb7(f"s_wo{i}", [128, KDI * 128], BF16) for i in range(2)]
            x3t = [sb7(f"s_x3{i}", [128, FQ], F32) for i in range(2)]
            smf = [sb7(f"s_smf{i}", [128, FQ], F32) for i in range(2)]
            outb = [sb7(f"s_out{t}", [128, DM], F32) for t in range(FQ // 128)]
            resb = sb7("s_res", [128, DM], F32)
            ps_o = [psum_(f"s_pso{i}", [128, PSF], F32) for i in range(2)]
            ps_t7 = [psum_(f"s_pst{i}", [128, PSF], F32) for i in range(2)]

            for fq in range(NQ):
                ygv = bass.AP(yg_a2ao.tensor, fq * FQ,
                              [[TOKC, 128], [128 * TOKC, KDI], [1, FQ]])
                S.dma("sync", ygt_all[:], ygv, ins=["dram_ygo"],
                      outs=["sygall"], extra=[a2a_tok], sem="dx")
                for m in range(MDM):
                    wb = m % 2
                    wk = f"swo{wb}"
                    wov = bass.AP(wo_ago.tensor, m * NDD * 128,
                                  [[LOR, 128], [128 * LOR, NC_], [1, NDD * 128]])
                    S.dma("sync", wot[wb][:, 0:KDI * 128], wov,
                          ins=["dram_woago"], outs=[wk], sem="dw")
                    xk3 = f"sx3{m % 2}"
                    S.dma("sync", x3t[m % 2][:],
                          x3_dram[m * 128:(m + 1) * 128, fq * FQ:(fq + 1) * FQ],
                          ins=["dram_x3"], outs=[xk3], sem="dld")
                    for f2 in range(NPF):
                        pb = f2 % 2
                        S.op("tensor", lambda e, wb=wb, f2=f2, fq=fq, pb=pb: [
                            e.matmul(out=ps_o[pb][:],
                                     lhsT=wot[wb][:, k * 128:(k + 1) * 128],
                                     rhs=ygt_all[:, k * FQ + f2 * PSF:
                                                 k * FQ + (f2 + 1) * PSF],
                                     start=(k == 0), stop=(k == KDI - 1))
                            for k in range(KDI)][-1],
                            ins=[wk, "sygall"],
                            outs=[f"spo{pb}"], sem="pe")
                        S.op("vector", lambda e, pb=pb, f2=f2, m=m:
                             e.tensor_tensor(
                                 out=smf[m % 2][:, f2 * PSF:(f2 + 1) * PSF],
                                 in0=ps_o[pb][:],
                                 in1=x3t[m % 2][:, f2 * PSF:(f2 + 1) * PSF],
                                 op=ALU.add),
                             ins=[f"spo{pb}", xk3], outs=[f"ssmf{m % 2}"])
                    for tg in range(FQ // PSF):
                        pb = tg % 2
                        pk = f"spt{pb}"
                        for j in range(PSF // 128):
                            tt = tg * (PSF // 128) + j
                            S.op("tensor", lambda e, tt=tt, j=j, pb=pb, m=m:
                                 e.transpose(
                                     out=ps_t7[pb][:, j * 128:(j + 1) * 128],
                                     in_=smf[m % 2][:, tt * 128:(tt + 1) * 128],
                                     identity=eyef_t[:]),
                                 ins=[f"ssmf{m % 2}", "eyef"], outs=[pk])
                        for j in range(PSF // 128):
                            tt = tg * (PSF // 128) + j
                            S.op("vector", lambda e, tt=tt, j=j, pb=pb, m=m:
                                 e.tensor_copy(
                                     out=outb[tt][:, m * 128:(m + 1) * 128],
                                     in_=ps_t7[pb][:, j * 128:(j + 1) * 128]),
                                 ins=[pk], outs=[f"sout{tt}"])
                for t in range(FQ // 128):
                    row0 = fq * FQ + t * 128
                    S.dma("sync", resb[:], x_tok[row0:row0 + 128, :],
                          outs=["sres"])
                    S.op("vector", lambda e, t=t: e.tensor_tensor(
                        out=outb[t][:], in0=outb[t][:], in1=resb[:], op=ALU.add),
                        ins=["sres", f"sout{t}"], outs=[f"sout{t}"])
                    S.dma("sync", out_ext[row0:row0 + 128, :], outb[t][:],
                          ins=[f"sout{t}"], outs=["dram_out"], sem="dst")

            if debug:
                for nm, (ap_out, src) in dbg.items():
                    S.dma("sync", ap_out, src, ins=[], outs=[],
                          extra=[S.current("dst"), S.current("cc"),
                                 S.current("dgp")], sem="dst")

            ftok = S.publish("dst", S.cnt["dst"])
            S.op("gpsimd", lambda e, v=ftok[1]: e.wait_ge(S.sems["dst_r"], v),
                 sem=None)

        with allow_ici_all_to_all(), nc.Block() as block:
            @block.sync
            def _(eng):
                for f in S.q["sync"]:
                    f(eng)

            @block.tensor
            def _(eng):
                for f in S.q["tensor"]:
                    f(eng)

            @block.vector
            def _(eng):
                for f in S.q["vector"]:
                    f(eng)

            @block.scalar
            def _(eng):
                for f in S.q["scalar"]:
                    f(eng)

            @block.gpsimd
            def _(eng):
                for f in S.q["gpsimd"]:
                    f(eng)

    return nc


def make_inputs(cfg, full_inputs):
    B, L, DM, DI = cfg["B"], cfg["L"], cfg["DM"], cfg["DI"]
    DS, DTR, HFF = cfg["DS"], cfg["DTR"], cfg["HFF"]
    TOK = B * L
    TOKC = TOK // NC_
    DCORE = DI // NC_
    HC = HFF // 128
    NSLOT = ceil_div(HC, NC_)
    fi = full_inputs
    x = np.ascontiguousarray(fi["hidden"].reshape(TOK, DM))
    w_in = fi["in_proj"]
    eye_b = np.eye(128, dtype=ml_dtypes.bfloat16)
    DS_ = cfg["DS"]
    sel = np.zeros((2 * DS_, 2 * DS_ * 128), dtype=ml_dtypes.bfloat16)
    for j_ in range(2 * DS_):
        sel[j_, j_ * 128:(j_ + 1) * 128] = 1
    eye_f = np.eye(128, dtype=np.float32)
    ones_b = np.ones((1, 128), dtype=ml_dtypes.bfloat16)
    ones_fr = np.ones((1, 128), dtype=np.float32)
    ones_fc = np.ones((128, 1), dtype=np.float32)
    maps = []
    for i in range(NC_):
        dsl = slice(i * DCORE, (i + 1) * DCORE)
        chunks = [c for c in range(HC) if c % NC_ == i]
        w1s = np.zeros((DM, NSLOT * 128), np.float32)
        w2s = np.zeros((DM, NSLOT * 128), np.float32)
        w3s = np.zeros((NSLOT * 128, DM), np.float32)
        for s, c in enumerate(chunks):
            w1s[:, s * 128:(s + 1) * 128] = fi["w1"][c * 128:(c + 1) * 128, :].T
            w2s[:, s * 128:(s + 1) * 128] = fi["w2"][c * 128:(c + 1) * 128, :].T
            w3s[s * 128:(s + 1) * 128, :] = fi["w3"][:, c * 128:(c + 1) * 128].T
        maps.append(dict(
            x_tok=np.ascontiguousarray(x[i * TOKC:(i + 1) * TOKC]),
            winT_x=np.ascontiguousarray(w_in[dsl].T),
            winT_z=np.ascontiguousarray(w_in[DI + i * DCORE:
                                             DI + (i + 1) * DCORE].T),
            convw=np.ascontiguousarray(fi["conv_w"][dsl, 0, :]),
            convb=np.ascontiguousarray(fi["conv_b"][dsl, None]),
            negA_in=np.ascontiguousarray(fi["negA"][dsl]),
            D_in=np.ascontiguousarray(fi["D"][dsl, None]),
            dtb_in=np.ascontiguousarray(fi["dt_b"][dsl, None]),
            dtwT=np.ascontiguousarray(fi["dt_w"][dsl].T),
            xpwT=np.ascontiguousarray(fi["xp_w"][:, dsl].T),
            woT_sl=np.ascontiguousarray(fi["out_w"][:, dsl].T),
            w1T_sl=w1s, w2T_sl=w2s, w3T_sl=w3s,
            ones_bf=ones_b, ones_fr=ones_fr, ones_fc=ones_fc,
            eye_bf=eye_b, eye_f=eye_f, sel_in=sel,
        ))
    return maps


def kernel(L0_attn_hidden_states, L0_attn_in_proj_w, L0_attn_conv_w,
           L0_attn_conv_b, L0_attn_x_proj_w, L0_attn_dt_proj_w,
           L0_attn_dt_proj_b, L0_attn_out_proj_w, L0_attn_negA, L0_attn_D,
           L0_ffn_w1, L0_ffn_w2, L0_ffn_w3):
    cfg = FULL
    fi = dict(hidden=np.asarray(L0_attn_hidden_states),
              in_proj=np.asarray(L0_attn_in_proj_w),
              conv_w=np.asarray(L0_attn_conv_w),
              conv_b=np.asarray(L0_attn_conv_b),
              xp_w=np.asarray(L0_attn_x_proj_w),
              dt_w=np.asarray(L0_attn_dt_proj_w),
              dt_b=np.asarray(L0_attn_dt_proj_b),
              out_w=np.asarray(L0_attn_out_proj_w),
              negA=np.asarray(L0_attn_negA),
              D=np.asarray(L0_attn_D),
              w1=np.asarray(L0_ffn_w1),
              w2=np.asarray(L0_ffn_w2),
              w3=np.asarray(L0_ffn_w3))
    nc = build_program(cfg)
    in_maps = make_inputs(cfg, fi)
    res = run_bass_kernel_spmd(nc, in_maps, core_ids=list(range(NC_)))
    out = np.concatenate([np.asarray(r["out_sl"]) for r in res.results], axis=0)
    B, L, DM = cfg["B"], cfg["L"], cfg["DM"]
    return out.reshape(B, L, DM).astype(np.float32)


# version 35
# speedup vs baseline: 1.0074x; 1.0074x over previous
import contextlib
import numpy as np
import ml_dtypes
import concourse.bass as bass
from concourse.aot_env import allow_ici_all_to_all
import concourse.mybir as mybir
from concourse.bass_utils import run_bass_kernel_spmd

F32 = mybir.dt.float32
BF16 = mybir.dt.bfloat16
AF = mybir.ActivationFunctionType
ALU = mybir.AluOpType
AX = mybir.AxisListType

NC_ = 8

FULL = dict(B=2, L=16384, DM=2048, DI=4096, DS=16, DTR=128, HFF=5632, LC=1024,
            PSF=512)
MINI = dict(B=2, L=1024, DM=256, DI=2048, DS=16, DTR=128, HFF=1280, LC=256,
            PSF=128)


class Buf:
    __slots__ = ("writer", "readers")

    def __init__(self):
        self.writer = None
        self.readers = []


class Sched:
    ENGINES = ("sync", "tensor", "vector", "scalar", "gpsimd")
    DMA_SEMS = {"dld": "sync", "dst": "gpsimd", "dgp": "gpsimd", "cc": "gpsimd",
                "dx": "sync", "dw": "sync"}

    def __init__(self, nc, stack):
        self.nc = nc
        self.q = {e: [] for e in self.ENGINES}
        self.sems = {}
        self.cnt = {}
        self.memo = {e: {} for e in self.ENGINES}
        self.bufs = {}
        self.pub = {}
        for s in ("pe", "dve", "act", "pool", "dld", "dst", "dgp", "cc",
                  "dx", "dw", "dld_r", "dst_r", "dgp_r", "cc_r", "dx_r", "dw_r"):
            self.sems[s] = stack.enter_context(nc.semaphore(s))
            self.cnt[s] = 0
        for s in self.DMA_SEMS:
            self.pub[s] = (0, 0)

    def publish(self, s, upto):
        cov, rv = self.pub[s]
        if cov >= upto:
            return (s + "_r", rv)
        eng = self.DMA_SEMS[s]
        tot = self.cnt[s]
        self._wait(eng, s, tot)
        rs = s + "_r"
        self.cnt[rs] += 1
        h = self.sems[rs]
        v = self.cnt[rs]
        self.q[eng].append(lambda e, h=h: e.sem_inc(h, 1))
        self.pub[s] = (tot, v)
        return (rs, v)

    def buf(self, key):
        if key not in self.bufs:
            self.bufs[key] = Buf()
        return self.bufs[key]

    def _wait(self, eng, sem, val):
        if val <= 0 or self.memo[eng].get(sem, 0) >= val:
            return
        self.memo[eng][sem] = val
        h = self.sems[sem]
        self.q[eng].append(lambda e, h=h, v=val: e.wait_ge(h, v))

    def op(self, eng, build, ins=(), outs=(), sem="", amt=1, extra=()):
        if sem == "":
            sem = {"tensor": "pe", "vector": "dve", "scalar": "act",
                   "gpsimd": "pool"}.get(eng)
        deps = {}

        def add(d):
            if d is None:
                return
            if d[0] in self.DMA_SEMS:
                d = self.publish(d[0], d[1])
            if deps.get(d[0], 0) < d[1]:
                deps[d[0]] = d[1]
        for k in ins:
            add(self.buf(k).writer)
        for k in outs:
            b = self.buf(k)
            add(b.writer)
            for r in b.readers:
                add(r)
        for d in extra:
            add(d)
        if eng == "tensor":
            deps.pop("pe", None)
        for s, v in deps.items():
            self._wait(eng, s, v)
        h = self.sems[sem] if sem else None
        if h is not None:
            self.cnt[sem] += amt
            self.q[eng].append(lambda e, b=build, h=h, a=amt: b(e).then_inc(h, a))
            tok = (sem, self.cnt[sem])
        else:
            self.q[eng].append(lambda e, b=build: b(e))
            tok = None
        for k in outs:
            b = self.buf(k)
            b.writer = tok
            b.readers = []
        if tok:
            for k in ins:
                self.buf(k).readers.append(tok)
        return tok

    def dma(self, eng, out_ap, in_ap, ins=(), outs=(), sem="", extra=(), **kw):
        if sem == "":
            sem = {"sync": "dld", "gpsimd": "dgp"}.get(eng, "dst")
        eng = self.DMA_SEMS[sem]
        return self.op(eng, lambda e: e.dma_start(out=out_ap, in_=in_ap, **kw),
                       ins=ins, outs=outs, sem=sem, amt=16, extra=extra)

    def current(self, sem):
        return (sem, self.cnt[sem])

    def fence(self, include_cc=False):
        toks = [(sm, self.cnt[sm]) for sm in ("pe", "dve", "act", "pool")]
        for sm in self.DMA_SEMS:
            if sm == "cc" and not include_cc:
                continue
            toks.append(self.publish(sm, self.cnt[sm]))
        for eng in self.ENGINES:
            for (sm, v) in toks:
                self._wait(eng, sm, v)


def ceil_div(a, b):
    return -(-a // b)


def build_program(cfg, debug=False):
    B, L, DM, DI = cfg["B"], cfg["L"], cfg["DM"], cfg["DI"]
    DS, DTR, HFF, LC, PSF = cfg["DS"], cfg["DTR"], cfg["HFF"], cfg["LC"], cfg["PSF"]
    TOK = B * L
    TOKC = TOK // NC_
    DCORE = DI // NC_
    KC = DM // 128
    NDC = DCORE // 128
    HC = HFF // 128
    NSLOT = ceil_div(HC, NC_)
    KDI = DI // 128
    NCH = TOK // LC
    NF = LC // PSF
    MDM = DM // 128
    TTOK = TOKC // 128
    NXP = DTR + 2 * DS
    assert LC % PSF == 0 and TOKC % LC == 0 and L % LC == 0 and DTR == 128

    nc = bass.Bass("TRN2", target_bir_lowering=False, debug=False)

    def din(name, shape, dt=F32):
        return nc.dram_tensor(name, shape, dt, kind="ExternalInput").ap()

    def dout(name, shape, dt=F32):
        return nc.dram_tensor(name, shape, dt, kind="ExternalOutput").ap()

    def dint(name, shape, dt):
        return nc.dram_tensor(name, shape, dt, kind="Internal").ap()

    x_tok = din("x_tok", [TOKC, DM])
    winT_x = din("winT_x", [DM, DCORE])
    winT_z = din("winT_z", [DM, DCORE])
    convw = din("convw", [DCORE, 4])
    convb = din("convb", [DCORE, 1])
    negA_in = din("negA_in", [DCORE, DS])
    D_in = din("D_in", [DCORE, 1])
    dtb_in = din("dtb_in", [DCORE, 1])
    dtwT = din("dtwT", [DTR, DCORE])
    xpwT = din("xpwT", [DCORE, NXP])
    woT_sl = din("woT_sl", [DCORE, DM])
    w1T_sl = din("w1T_sl", [DM, NSLOT * 128])
    w2T_sl = din("w2T_sl", [DM, NSLOT * 128])
    w3T_sl = din("w3T_sl", [NSLOT * 128, DM])
    ones_bf = din("ones_bf", [1, 128], BF16)
    ones_fr = din("ones_fr", [1, 128])
    ones_fc = din("ones_fc", [128, 1])
    eye_bf = din("eye_bf", [128, 128], BF16)
    sel_in = din("sel_in", [2 * DS, 2 * DS * 128], BF16)
    eye_f = din("eye_f", [128, 128])
    out_ext = dout("out_sl", [TOKC, DM])

    NQX = max(1, TOKC // max(LC, PSF * 2, TOKC // 4))
    QTOK = TOKC // NQX
    xn_agi = dint("xn_agi", [NQX * DM, QTOK], BF16)
    xn_ago = dint("xn_ago", [NC_ * NQX * DM, QTOK], BF16)
    gp_ari = dint("gp_ari", [1, 4], F32)
    gp_aro = dint("gp_aro", [1, 4], F32)
    NDD = DCORE // 128
    L1R = NSLOT * KC * 128
    L3R = MDM * NSLOT * 128
    LOR = MDM * NDD * 128
    w1_agi = dint("w1_agi", [128, L1R], BF16)
    w1_ago = dint("w1_ago", [NC_ * 128, L1R], BF16)
    w2_agi = dint("w2_agi", [128, L1R], BF16)
    w2_ago = dint("w2_ago", [NC_ * 128, L1R], BF16)
    w3_agi = dint("w3_agi", [128, L3R], BF16)
    w3_ago = dint("w3_ago", [NC_ * 128, L3R], BF16)
    wo_agi = dint("wo_agi", [128, LOR], BF16)
    wo_ago = dint("wo_ago", [NC_ * 128, LOR], BF16)
    LW = 2 * NDC * KC * 128
    win_img = dint("win_img", [128, LW], BF16)
    xdbl_p = dint("xdbl_p", [NCH * NXP, LC], BF16)
    xdbl_f = dint("xdbl_f", [NCH * NXP, LC], BF16)
    hid_dram = dint("hid_dram", [HFF, TOKC], BF16)
    x3_dram = dint("x3_dram", [DM, TOKC], F32)
    yg_a2ai = dint("yg_a2ai", [NC_ * DCORE, TOKC], BF16)
    yg_a2ao = dint("yg_a2ao", [NC_ * DCORE, TOKC], BF16)

    if debug:
        dbg = {}
        for nm, src, shape, dt in [
                ("dbg_xn", xn_agi, [NQX * DM, QTOK], BF16),
                ("dbg_xdbl", xdbl_f, [NCH * NXP, LC], BF16),
                ("dbg_yg", yg_a2ai, [NC_ * DCORE, TOKC], BF16),
                ("dbg_hid", hid_dram, [HFF, TOKC], BF16),
                ("dbg_x3", x3_dram, [DM, TOKC], F32),
                ("dbg_w1q", w1_ago, [NC_ * 128, NSLOT * KC * 128], BF16)]:
            dbg[nm] = (dout(nm, shape, dt), src)

    rg = [list(range(NC_))]
    ar_tok = {}

    with contextlib.ExitStack() as stack:
        S = Sched(nc, stack)

        def sb(name, shape, dt):
            return stack.enter_context(nc.sbuf_tensor(name, shape, dt))

        def mk_psum(st):
            def f(name, shape, dt=F32):
                return st.enter_context(nc.psum_tensor(name, shape, dt))
            return f

        eyeb_t = sb("eyeb_t", [128, 128], BF16)
        sel_t = sb("sel_t", [2 * DS, 2 * DS * 128], BF16)
        eyef_t = sb("eyef_t", [128, 128], F32)
        onesb_t = sb("onesb_t", [1, 128], BF16)
        onesfr_t = sb("onesfr_t", [1, 128], F32)
        onesfc_t = sb("onesfc_t", [128, 1], F32)
        negA_t = sb("negA_t", [128, NDC * DS], F32)
        convw_t = sb("convw_t", [128, NDC * 4], F32)
        convb_t = sb("convb_t", [128, NDC], F32)
        D_t = sb("D_t", [128, NDC], F32)
        dtb_t = sb("dtb_t", [128, NDC], F32)
        dtw_t = sb("dtw_t", [128, DCORE], BF16)
        xpw_t = sb("xpw_t", [128, NDC * NXP], BF16)
        state_t = sb("state_t", [128, NDC * DS], F32)
        ginv_t = sb("ginv_t", [128, 4], F32)

        S.dma("sync", eyeb_t[:], eye_bf, outs=["eyeb"])
        S.dma("sync", sel_t[:], sel_in, outs=["selt"])
        S.dma("sync", eyef_t[:], eye_f, outs=["eyef"])
        S.dma("sync", onesb_t[:], ones_bf, outs=["onesb"])
        S.dma("sync", onesfr_t[:], ones_fr, outs=["onesfr"])
        S.dma("sync", onesfc_t[:], ones_fc, outs=["onesfc"])
        for dc in range(NDC):
            S.dma("sync", negA_t[:, dc * DS:(dc + 1) * DS],
                  negA_in[dc * 128:(dc + 1) * 128, :], outs=["negA"])
            S.dma("sync", convw_t[:, dc * 4:(dc + 1) * 4],
                  convw[dc * 128:(dc + 1) * 128, :], outs=["convw"])
            S.dma("sync", convb_t[:, dc:dc + 1],
                  convb[dc * 128:(dc + 1) * 128, :], outs=["convb"])
            S.dma("sync", D_t[:, dc:dc + 1],
                  D_in[dc * 128:(dc + 1) * 128, :], outs=["Dt"])
            S.dma("sync", dtb_t[:, dc:dc + 1],
                  dtb_in[dc * 128:(dc + 1) * 128, :], outs=["dtb"])

        with contextlib.ExitStack() as ph:
            def sb0(name, shape, dt):
                return ph.enter_context(nc.sbuf_tensor(name, shape, dt))
            psum_ = mk_psum(ph)
            xt = [sb0(f"p0_xt{i}", [128, DM], F32) for i in range(4)]
            xnb = [sb0(f"p0_xnb{i}", [128, DM], BF16) for i in range(4)]
            red2 = [sb0(f"p0_red{i}", [128, 8], F32) for i in range(2)]
            sq2 = [sb0(f"p0_sq{i}", [128, DM], F32) for i in range(2)]
            evb = [sb0(f"p0_evb{i}", [128, PSF], BF16) for i in range(2)]
            ps_tr = [psum_(f"p0_pstr{i}", [128, PSF], BF16) for i in range(2)]
            WMAX = max(DM, NSLOT * 128)
            wt2 = [sb0(f"p0_wt{i}", [128, WMAX], F32) for i in range(4)]
            WSEM = ["dld", "dgp", "dld", "dgp"]
            wq_bf = sb0("p0_wqb", [128, WMAX], BF16)
            gpos = sb0("p0_gpos", [128, WMAX], F32)
            TRP = PSF // 128

            def xload(t):
                b = t % 4
                S.dma("sync", xt[b][:], x_tok[t * 128:(t + 1) * 128, :],
                      outs=[f"p0x{b}"], sem=("dld" if b % 2 == 0 else "dgp"))
            for t in range(min(3, TTOK)):
                xload(t)
            for t in range(TTOK):
                b = t % 4
                pr = t % 2
                red = red2[pr]
                sq = sq2[pr]
                rk = f"p0red{pr}"
                xk = f"p0x{b}"
                if t + 3 < TTOK:
                    xload(t + 3)
                S.op("vector", lambda e, b=b, red=red: e.tensor_reduce(
                    out=red[:, 0:1], in_=xt[b][:], axis=AX.X, op=ALU.add),
                    ins=[xk], outs=[rk])
                S.op("scalar", lambda e, b=b, red=red, sq=sq: e.activation(
                    out=sq[:], in_=xt[b][:], func=AF.Square, accum_out=red[:, 1:2]),
                    ins=[xk], outs=[f"p0sq{pr}", rk])
                S.op("vector", lambda e, red=red: e.tensor_scalar(
                    out=red[:, 2:3], in0=red[:, 0:1], scalar1=1.0 / DM,
                    scalar2=None, op0=ALU.mult), ins=[rk], outs=[rk])
                S.op("vector", lambda e, red=red: e.tensor_tensor(
                    out=red[:, 3:4], in0=red[:, 2:3], in1=red[:, 2:3], op=ALU.mult),
                    ins=[rk], outs=[rk])
                S.op("vector", lambda e, red=red: e.tensor_scalar(
                    out=red[:, 4:5], in0=red[:, 1:2], scalar1=1.0 / DM,
                    scalar2=None, op0=ALU.mult), ins=[rk], outs=[rk])
                S.op("vector", lambda e, red=red: e.tensor_tensor(
                    out=red[:, 5:6], in0=red[:, 4:5], in1=red[:, 3:4],
                    op=ALU.subtract), ins=[rk], outs=[rk])
                S.op("vector", lambda e, red=red: e.tensor_scalar(
                    out=red[:, 5:6], in0=red[:, 5:6], scalar1=1e-5, scalar2=None,
                    op0=ALU.add), ins=[rk], outs=[rk])
                S.op("vector", lambda e, red=red: e.reciprocal(
                    out=red[:, 6:7], in_=red[:, 5:6]), ins=[rk], outs=[rk])
                S.op("scalar", lambda e, red=red: e.activation(
                    out=red[:, 7:8], in_=red[:, 6:7], func=AF.Sqrt),
                    ins=[rk], outs=[rk])
                xnk = f"p0xn{t % 4}"
                S.op("vector", lambda e, b=b, t=t, red=red: e.tensor_scalar(
                    out=xnb[t % 4][:], in0=xt[b][:], scalar1=red[:, 2:3],
                    scalar2=red[:, 7:8], op0=ALU.subtract, op1=ALU.mult),
                    ins=[xk, rk], outs=[xnk])
                if t % TRP == TRP - 1:
                    g = t // TRP
                    pb = g % 2
                    pk = f"p0ps{pb}"
                    ek = f"p0ev{pb}"
                    for d_c in range(KC):
                        for j in range(TRP):
                            tt = g * TRP + j
                            S.op("tensor", lambda e, d_c=d_c, j=j, tt=tt, pb=pb:
                                 e.transpose(
                                     out=ps_tr[pb][:, j * 128:(j + 1) * 128],
                                     in_=xnb[tt % 4][:, d_c * 128:(d_c + 1) * 128],
                                     identity=eyeb_t[:]),
                                 ins=[f"p0xn{tt % 4}", "eyeb"], outs=[pk])
                        S.op("scalar", lambda e, pb=pb: e.activation(
                            out=evb[pb][:], in_=ps_tr[pb][:], func=AF.Identity),
                            ins=[pk], outs=[ek])
                        qq = (g * PSF) // QTOK
                        qc = (g * PSF) % QTOK
                        S.dma("sync", xn_agi[qq * DM + d_c * 128:
                                             qq * DM + (d_c + 1) * 128,
                                             qc:qc + PSF],
                              evb[pb][:], ins=[ek], outs=[f"dram_xnagi{qq}"],
                              sem="dst")

            for qq in range(NQX):
                S.op("gpsimd", lambda e, qq=qq: e.collective_compute(
                    "AllGather", ALU.bypass, replica_groups=rg,
                    ins=[xn_agi[qq * DM:(qq + 1) * DM, :].opt()],
                    outs=[xn_ago[qq * NC_ * DM:(qq + 1) * NC_ * DM, :].opt()]),
                    ins=[f"dram_xnagi{qq}"], outs=[f"dram_xnago{qq}"], sem="cc",
                    extra=[S.current("dst")])
            for half, wsrc in ((0, winT_x), (1, winT_z)):
                for kc in range(KC):
                    wb_ = kc % 2
                    S.dma("sync", wt2[wb_][:, 0:DCORE],
                          wsrc[kc * 128:(kc + 1) * 128, :],
                          outs=[f"p0wt{wb_}"], sem=("dld" if wb_ == 0 else "dx"))
                    S.op("vector", lambda e, wb_=wb_: e.tensor_copy(
                        out=wq_bf[:, 0:DCORE], in_=wt2[wb_][:, 0:DCORE]),
                        ins=[f"p0wt{wb_}"], outs=["p0wqb"])
                    out_ap = bass.AP(win_img.tensor,
                                     (half * NDC * KC + kc) * 128,
                                     [[LW, 128], [KC * 128, NDC], [1, 128]])
                    S.dma("sync", out_ap, wq_bf[:, 0:DCORE],
                          ins=["p0wqb"], outs=["dram_winimg"], sem="dst")
            S.dma("sync", wt2[0][:, 0:DCORE], dtwT, outs=["p0wt0"])
            S.op("vector", lambda e: e.tensor_copy(
                out=dtw_t[:], in_=wt2[0][:, 0:DCORE]), ins=["p0wt0"], outs=["dtw"])
            for dc in range(NDC):
                wb_ = dc % 2
                S.dma("sync", wt2[wb_][:, 0:NXP], xpwT[dc * 128:(dc + 1) * 128, :],
                      outs=[f"p0wt{wb_}"], sem=("dld" if wb_ == 0 else "dx"))
                S.op("vector", lambda e, dc=dc, wb_=wb_: e.tensor_copy(
                    out=xpw_t[:, dc * NXP:(dc + 1) * NXP],
                    in_=wt2[wb_][:, 0:NXP]),
                    ins=[f"p0wt{wb_}"], outs=["xpw"])

            wabs = sb0("p0_wabs", [128, WMAX], F32)
            acc3 = sb0("p0_acc3", [128, 4], F32)
            accr = sb0("p0_accr", [128, 1], F32)
            ps_g = psum_("p0_psg", [4, 1], F32)
            gp_sb = sb0("p0_gp", [1, 4], F32)
            gpc_sb = sb0("p0_gpc", [4, 1], F32)
            ps_gb = psum_("p0_psgb", [128, 4], F32)
            S.op("vector", lambda e: e.memset(acc3[:], 0.0), outs=["p0acc3"])
            wsl = [(w1T_sl, DM, NSLOT * 128, 0, w1_agi),
                   (w2T_sl, DM, NSLOT * 128, 1, w2_agi),
                   (w3T_sl, NSLOT * 128, DM, 2, w3_agi)]
            wtiles = []
            for (wsrc, rows, cols, j, _agi) in wsl:
                for r in range(rows // 128):
                    wtiles.append((wsrc, r, cols, j))

            def wload(i):
                wsrc, r, cols, j = wtiles[i]
                wb_ = i % 4
                S.dma("sync", wt2[wb_][:, 0:cols],
                      wsrc[r * 128:(r + 1) * 128, :],
                      outs=[f"p0wt{wb_}"], sem=WSEM[wb_])
            for i in range(min(3, len(wtiles))):
                wload(i)
            for i in range(len(wtiles)):
                wsrc, r, cols, j = wtiles[i]
                wb_ = i % 4
                S.op("scalar", lambda e, cols=cols, wb_=wb_: e.activation(
                    out=wabs[:, 0:cols], in_=wt2[wb_][:, 0:cols], func=AF.Abs,
                    accum_out=accr[:]), ins=[f"p0wt{wb_}"],
                    outs=["p0wabs", "p0accr"])
                S.op("vector", lambda e, j=j: e.tensor_tensor(
                    out=acc3[:, j:j + 1], in0=acc3[:, j:j + 1], in1=accr[:],
                    op=ALU.add), ins=["p0accr"], outs=["p0acc3"])
                if i + 3 < len(wtiles):
                    wload(i + 3)
            S.op("tensor", lambda e: e.matmul(
                out=ps_g[:], lhsT=acc3[:, 0:4], rhs=onesfc_t[:], start=True,
                stop=True), ins=["p0acc3", "onesfc"], outs=["p0psg"])
            S.op("vector", lambda e: e.tensor_copy(out=gpc_sb[:], in_=ps_g[:]),
                 ins=["p0psg"], outs=["p0gpc"])
            S.dma("sync", gp_ari.rearrange("a b -> b a"), gpc_sb[:],
                  ins=["p0gpc"], outs=["dram_gpi"], sem="dst")
            S.op("gpsimd", lambda e: e.collective_compute(
                "AllReduce", ALU.add, replica_groups=rg,
                ins=[gp_ari.opt()], outs=[gp_aro.opt()]),
                ins=["dram_gpi"], outs=["dram_gpo"], sem="cc",
                extra=[S.current("dst")])
            NW = float(DM * HFF)
            S.dma("sync", gp_sb[:], gp_aro, ins=["dram_gpo"], outs=["p0gp"])
            S.op("vector", lambda e: e.tensor_scalar(
                out=gp_sb[:], in0=gp_sb[:], scalar1=1.0 / NW, scalar2=1e-8,
                op0=ALU.mult, op1=ALU.add), ins=["p0gp"], outs=["p0gp"])
            S.op("vector", lambda e: e.reciprocal(out=gp_sb[:], in_=gp_sb[:]),
                 ins=["p0gp"], outs=["p0gp"])
            S.op("tensor", lambda e: e.matmul(
                out=ps_gb[:], lhsT=onesfr_t[:], rhs=gp_sb[:], start=True,
                stop=True), ins=["p0gp", "onesfr"], outs=["p0psgb"])
            S.op("vector", lambda e: e.tensor_copy(out=ginv_t[:], in_=ps_gb[:]),
                 ins=["p0psgb"], outs=["ginv"])

            qtiles = []
            for (wsrc, rows, cols, j, agi) in wsl:
                for r in range(rows // 128):
                    qtiles.append((wsrc, r, cols, j, agi))

            def qload(i):
                wsrc, r, cols, j, agi = qtiles[i]
                wb_ = i % 4
                S.dma("sync", wt2[wb_][:, 0:cols],
                      wsrc[r * 128:(r + 1) * 128, :],
                      outs=[f"p0wt{wb_}"], sem=WSEM[wb_])
            for i in range(min(3, len(qtiles))):
                qload(i)
            for i in range(len(qtiles)):
                wsrc, r, cols, j, agi = qtiles[i]
                wb_ = i % 4
                if True:
                    S.op("vector", lambda e, cols=cols, j=j, wb_=wb_:
                         e.tensor_scalar(
                             out=wabs[:, 0:cols], in0=wt2[wb_][:, 0:cols],
                             scalar1=ginv_t[:, j:j + 1], scalar2=None,
                             op0=ALU.mult),
                        ins=[f"p0wt{wb_}", "ginv"], outs=["p0wabs"])
                    S.op("vector", lambda e, cols=cols: e.tensor_scalar(
                        out=gpos[:, 0:cols], in0=wabs[:, 0:cols], scalar1=0.5,
                        scalar2=None, op0=ALU.is_gt), ins=["p0wabs"],
                        outs=["p0gpos"])
                    S.op("vector", lambda e, cols=cols: e.tensor_scalar(
                        out=wabs[:, 0:cols], in0=wabs[:, 0:cols], scalar1=-0.5,
                        scalar2=None, op0=ALU.is_lt), ins=["p0wabs"],
                        outs=["p0wabs"])
                    S.op("vector", lambda e, cols=cols: e.tensor_tensor(
                        out=wq_bf[:, 0:cols], in0=gpos[:, 0:cols],
                        in1=wabs[:, 0:cols], op=ALU.subtract),
                        ins=["p0gpos", "p0wabs"], outs=["p0wqb"])
                    if j < 2:
                        out_ap = bass.AP(agi.tensor, r * 128,
                                         [[L1R, 128], [KC * 128, NSLOT], [1, 128]])
                        S.dma("sync", out_ap, wq_bf[:, 0:NSLOT * 128],
                              ins=["p0wqb"], outs=[f"dram_wagi{j}"], sem="dst")
                    else:
                        out_ap = bass.AP(agi.tensor, r * 128,
                                         [[L3R, 128], [NSLOT * 128, MDM], [1, 128]])
                        S.dma("sync", out_ap, wq_bf[:, 0:DM],
                              ins=["p0wqb"], outs=[f"dram_wagi{j}"], sem="dst")
                    if i + 3 < len(qtiles):
                        qload(i + 3)
            for j, (agi, ago) in enumerate([(w1_agi, w1_ago), (w2_agi, w2_ago),
                                            (w3_agi, w3_ago)]):
                S.op("gpsimd", lambda e, agi=agi, ago=ago: e.collective_compute(
                    "AllGather", ALU.bypass, replica_groups=rg,
                    ins=[agi.opt()], outs=[ago.opt()]),
                    ins=[f"dram_wagi{j}"], outs=[f"dram_wago{j}"], sem="cc",
                    extra=[S.current("dst")])

            for r in range(DCORE // 128):
                wb_ = r % 2
                S.dma("sync", wt2[wb_][:, 0:DM], woT_sl[r * 128:(r + 1) * 128, :],
                      outs=[f"p0wt{wb_}"], sem=("dld" if wb_ == 0 else "dx"))
                S.op("vector", lambda e, wb_=wb_: e.tensor_copy(
                    out=wq_bf[:, 0:DM], in_=wt2[wb_][:, 0:DM]),
                    ins=[f"p0wt{wb_}"], outs=["p0wqb"])
                out_ap = bass.AP(wo_agi.tensor, r * 128,
                                 [[LOR, 128], [NDD * 128, MDM], [1, 128]])
                S.dma("sync", out_ap, wq_bf[:, 0:DM],
                      ins=["p0wqb"], outs=["dram_woagi"], sem="dst")
            S.op("gpsimd", lambda e: e.collective_compute(
                "AllGather", ALU.bypass, replica_groups=rg,
                ins=[wo_agi.opt()], outs=[wo_ago.opt()]),
                ins=["dram_woagi"], outs=["dram_woago"], sem="cc",
                extra=[S.current("dst")])

            S.op("vector", lambda e: e.memset(state_t[:], 0.0), outs=["state"])

        S.fence()
        with contextlib.ExitStack() as ph:
            def sbM(name, shape, dt):
                return ph.enter_context(nc.sbuf_tensor(name, shape, dt))
            psum_ = mk_psum(ph)
            HLC = LC // 2
            win2 = [sbM(f"m_win{i}", [128, KC * 128], BF16) for i in range(2)]
            xnm = [sbM(f"m_xn{h}", [128, KC * HLC], BF16) for h in range(2)]
            xp_ = [sbM(f"m_xp{dc}", [128, LC + 3], BF16) for dc in range(NDC)]
            hsv = sbM("m_hsv", [128, 4 * NDC], BF16)
            cv1 = sbM("m_cv1", [128, LC], BF16)
            cv2 = sbM("m_cv2", [128, LC], BF16)
            u_b = [[sbM(f"m_u{dc}_{i}", [128, LC], BF16) for i in range(2)]
                   for dc in range(NDC)]
            sz_b = [[sbM(f"m_sz{dc}_{i}", [128, LC], BF16) for i in range(2)]
                    for dc in range(NDC)]
            xdt_b = sbM("m_xdt", [128, LC], BF16)
            xbc_b = sbM("m_xbc", [32, LC], BF16)
            dt_bf = [sbM(f"d_dtbf{i}", [128, LC], BF16) for i in range(2)]
            bc32b = [sbM(f"d_bc32b{i}", [32, LC], BF16) for i in range(2)]
            bbc = [sbM(f"d_bbc{i}", [128, LC], BF16) for i in range(2)]
            cbc = [sbM(f"d_cbc{i}", [128, LC], BF16) for i in range(2)]
            delta = [[sbM(f"d_delta{dc}_{i}", [128, LC], BF16) for i in range(2)]
                     for dc in range(NDC)]
            du_b = [[sbM(f"d_du{dc}_{i}", [128, LC], BF16) for i in range(2)]
                    for dc in range(NDC)]
            yacc = [sbM(f"d_yacc{dc}", [128, LC], BF16) for dc in range(NDC)]
            sc1 = sbM("d_sc1", [128, PSF], BF16)
            dA_ = [sbM(f"d_dA{i}", [128, LC], BF16) for i in range(2)]
            dBu_ = [sbM(f"d_dBu{i}", [128, LC], BF16) for i in range(2)]
            so_ = [sbM(f"d_so{i}", [128, LC], BF16) for i in range(2)]
            FHT = max(TOKC // 8, PSF)
            NFH = TOKC // FHT
            NFF = FHT // PSF
            xnf = sbM("f_xnf", [128, KC * FHT], BF16)
            w12 = [sbM(f"f_w{i}", [128, 2 * KC * 128], BF16) for i in range(2)]
            tb_ = [sbM(f"f_tb{i}", [128, PSF], BF16) for i in range(2)]
            x2h = [sbM(f"f_x2h{i}", [128, PSF], BF16) for i in range(2)]
            hb_ = [sbM(f"f_hb{i}", [128, PSF], BF16) for i in range(2)]
            psA = psum_("m_psA", [128, PSF], F32)
            psB = psum_("m_psB", [128, PSF], F32)
            psC = psum_("m_psC", [128, PSF], F32)
            psD = psum_("m_psD", [128, PSF], F32)
            psE = psum_("m_psE", [128, PSF], F32)
            psF1 = psum_("m_psF1", [128, PSF], F32)
            psF2 = psum_("m_psF2", [128, PSF], F32)
            psF3 = psum_("m_psF3", [128, PSF], F32)

            def emit_p1_start(c):
                g0 = c * LC
                r0 = g0 // TOKC
                col0 = g0 % TOKC
                qq = col0 // QTOK
                qc = col0 % QTOK
                base = (qq * NC_ + r0) * DM
                for h in range(2):
                    xv = bass.AP(xn_ago.tensor,
                                 base * QTOK + qc + h * HLC,
                                 [[QTOK, 128], [128 * QTOK, KC], [1, HLC]])
                    S.dma("sync", xnm[h][:], xv,
                          ins=[f"dram_xnago{qq}"], outs=[f"p1xn{h}"], sem="dx")

            def emit_p1_m(c, m):
                cb_ = c % 2
                half = m // NDC
                dc = m % NDC
                wmb = m % 2
                wmk = f"p1w{wmb}"
                S.dma("sync", win2[wmb][:],
                      win_img[:, m * KC * 128:(m + 1) * KC * 128],
                      ins=["dram_winimg"], outs=[wmk], sem="dw")
                for h in range(2):
                    pb = (m * 2 + h) % 2
                    psx = psA if pb == 0 else psB
                    pk = f"p1ps{pb}"
                    S.op("tensor", lambda e, m=m, h=h, wmb=wmb, psx=psx: [
                        e.matmul(out=psx[:, 0:HLC],
                                 lhsT=win2[wmb][:, k * 128:(k + 1) * 128],
                                 rhs=xnm[h][:, k * HLC:(k + 1) * HLC],
                                 start=(k == 0), stop=(k == KC - 1))
                        for k in range(KC)][-1],
                        ins=[wmk, f"p1xn{h}"],
                        outs=[pk], sem="pe")
                    if half == 0:
                        colf = 3 + h * HLC
                        S.op("scalar", lambda e, dc=dc, psx=psx, colf=colf:
                             e.activation(
                                 out=xp_[dc][:, colf:colf + HLC],
                                 in_=psx[:, 0:HLC], func=AF.Identity),
                             ins=[pk], outs=[f"p1xp{dc}"])
                    else:
                        S.op("scalar", lambda e, dc=dc, h=h, psx=psx, cb_=cb_:
                             e.activation(
                                 out=sz_b[dc][cb_][:, h * HLC:(h + 1) * HLC],
                                 in_=psx[:, 0:HLC], func=AF.Silu),
                             ins=[pk], outs=[f"p1sz{dc}_{cb_}"])

            def emit_p1_conv(c):
                g0 = c * LC
                cb_ = c % 2
                batch_first = (g0 % L) == 0
                for dc in range(NDC):
                    xpk = f"p1xp{dc}"
                    if batch_first:
                        S.op("vector", lambda e, dc=dc: e.memset(
                            xp_[dc][:, 0:3], 0.0), outs=[xpk])
                    else:
                        S.op("vector", lambda e, dc=dc: e.tensor_copy(
                             out=xp_[dc][:, 0:3],
                             in_=hsv[:, dc * 4:dc * 4 + 3]),
                             ins=["p1hsv"], outs=[xpk])
                    for tap in range(4):
                        dstb = cv1 if tap == 0 else cv2
                        dk = "p1cv1" if tap == 0 else "p1cv2"
                        S.op("scalar", lambda e, dc=dc, tap=tap,
                             d=dstb: e.activation(
                                 out=d[:], in_=xp_[dc][:, tap:tap + LC],
                                 func=AF.Identity,
                                 scale=convw_t[:, dc * 4 + tap:
                                               dc * 4 + tap + 1]),
                            ins=[xpk, "convw"], outs=[dk])
                        if tap > 0:
                            S.op("vector", lambda e: e.tensor_tensor(
                                out=cv1[:], in0=cv1[:], in1=cv2[:], op=ALU.add),
                                ins=["p1cv1", "p1cv2"], outs=["p1cv1"])
                    S.op("vector", lambda e, dc=dc: e.tensor_copy(
                        out=hsv[:, dc * 4:dc * 4 + 3],
                        in_=xp_[dc][:, LC:LC + 3]), ins=[xpk], outs=["p1hsv"])
                    S.op("scalar", lambda e, dc=dc, cb_=cb_: e.activation(
                        out=u_b[dc][cb_][:], in_=cv1[:], func=AF.Silu,
                        bias=convb_t[:, dc:dc + 1]),
                        ins=["p1cv1", "convb"], outs=[f"p1u{dc}_{cb_}"])
                for f in range(NF):
                    for dc in range(NDC):
                        S.op("tensor", lambda e, dc=dc, f=f, cb_=cb_: e.matmul(
                            out=psA[:],
                            lhsT=xpw_t[:, dc * NXP:dc * NXP + 128],
                            rhs=u_b[dc][cb_][:, f * PSF:(f + 1) * PSF],
                            start=(dc == 0), stop=(dc == NDC - 1)),
                            ins=["xpw", f"p1u{dc}_{cb_}"], outs=["p1ps0"])
                        S.op("tensor", lambda e, dc=dc, f=f, cb_=cb_: e.matmul(
                            out=psB[0:32, :],
                            lhsT=xpw_t[:, dc * NXP + 128:dc * NXP + NXP],
                            rhs=u_b[dc][cb_][:, f * PSF:(f + 1) * PSF],
                            start=(dc == 0), stop=(dc == NDC - 1)),
                            ins=["xpw", f"p1u{dc}_{cb_}"], outs=["p1ps1"])
                    S.op("scalar", lambda e, f=f: e.activation(
                        out=xdt_b[:, f * PSF:(f + 1) * PSF], in_=psA[:],
                        func=AF.Identity), ins=["p1ps0"], outs=["p1xdt"])
                    S.op("scalar", lambda e, f=f: e.activation(
                        out=xbc_b[:, f * PSF:(f + 1) * PSF], in_=psB[0:32, :],
                        func=AF.Identity), ins=["p1ps1"], outs=["p1xbc"])
                S.dma("sync", xdbl_p[c * NXP:c * NXP + 128, :], xdt_b[:],
                      ins=["p1xdt"], outs=[f"dram_xdp_{c}"], sem="dst")
                S.dma("sync", xdbl_p[c * NXP + 128:(c + 1) * NXP, :], xbc_b[:],
                      ins=["p1xbc"], outs=[f"dram_xdp_{c}"], sem="dst")
                ar_tok[c] = S.op("gpsimd", lambda e, c=c: e.collective_compute(
                    "AllReduce", ALU.add, replica_groups=rg,
                    ins=[xdbl_p[c * NXP:(c + 1) * NXP, :].opt()],
                    outs=[xdbl_f[c * NXP:(c + 1) * NXP, :].opt()]),
                    ins=[f"dram_xdp_{c}"], outs=[f"dram_xdf_{c}"], sem="cc",
                    extra=[S.current("dst")])

            def emit_d_load(c):
                cb_ = c % 2
                S.dma("sync", dt_bf[cb_][:], xdbl_f[c * NXP:c * NXP + 128, :],
                      ins=[f"dram_xdf_{c}"], outs=[f"d_dtbf{cb_}"], sem="dx")
                S.dma("sync", bc32b[cb_][:],
                      xdbl_f[c * NXP + 128:(c + 1) * NXP, :],
                      ins=[f"dram_xdf_{c}"], outs=[f"d_bc32b{cb_}"], sem="dx")

            def emit_d_head(c):
                g0 = c * LC
                cb_ = c % 2
                for dc in range(NDC):
                    for f in range(NF):
                        S.op("tensor", lambda e, dc=dc, f=f, cb_=cb_: e.matmul(
                            out=psC[:],
                            lhsT=dtw_t[:, dc * 128:(dc + 1) * 128],
                            rhs=dt_bf[cb_][:, f * PSF:(f + 1) * PSF],
                            start=True, stop=True),
                            ins=["dtw", f"d_dtbf{cb_}"], outs=["d_pC"])
                        S.op("scalar", lambda e, dc=dc, f=f: e.activation(
                            out=sc1[:], in_=psC[:], func=AF.Exp,
                            bias=dtb_t[:, dc:dc + 1]),
                            ins=["d_pC", "dtb"], outs=["d_sc1"])
                        S.op("scalar", lambda e, dc=dc, f=f, cb_=cb_:
                             e.activation(
                                 out=delta[dc][cb_][:, f * PSF:(f + 1) * PSF],
                                 in_=sc1[:], func=AF.Ln, bias=1.0),
                            ins=["d_sc1"], outs=[f"d_del{dc}_{cb_}"])
                    S.op("vector", lambda e, dc=dc, cb_=cb_: e.tensor_tensor(
                        out=du_b[dc][cb_][:], in0=delta[dc][cb_][:],
                        in1=u_b[dc][cb_][:],
                        op=ALU.mult), ins=[f"d_del{dc}_{cb_}", f"p1u{dc}_{cb_}"],
                        outs=[f"d_du{dc}_{cb_}"])
                if (g0 % L) == 0:
                    S.op("vector", lambda e: e.memset(state_t[:], 0.0),
                         outs=["state"])

            def emit_d_scan(c, inject=None):
                g0 = c * LC
                cb_ = c % 2
                for n in range(DS):
                    if inject:
                        inject(n)
                    nb = n % 2
                    bbk, cbk = f"d_bbc{nb}", f"d_cbc{nb}"
                    for f in range(NF):
                        S.op("tensor", lambda e, n=n, f=f, cb_=cb_: e.matmul(
                            out=psD[:], lhsT=sel_t[:, n * 128:(n + 1) * 128],
                            rhs=bc32b[cb_][:, f * PSF:(f + 1) * PSF],
                            start=True, stop=True),
                            ins=[f"d_bc32b{cb_}", "selt"], outs=["d_pD"])
                        S.op("scalar", lambda e, nb=nb, f=f: e.activation(
                            out=bbc[nb][:, f * PSF:(f + 1) * PSF],
                            in_=psD[:], func=AF.Identity),
                            ins=["d_pD"], outs=[bbk])
                        S.op("tensor", lambda e, n=n, f=f, cb_=cb_: e.matmul(
                            out=psE[:],
                            lhsT=sel_t[:, (DS + n) * 128:(DS + n + 1) * 128],
                            rhs=bc32b[cb_][:, f * PSF:(f + 1) * PSF],
                            start=True, stop=True),
                            ins=[f"d_bc32b{cb_}", "selt"], outs=["d_pE"])
                        S.op("scalar", lambda e, nb=nb, f=f: e.activation(
                            out=cbc[nb][:, f * PSF:(f + 1) * PSF],
                            in_=psE[:], func=AF.Identity),
                            ins=["d_pE"], outs=[cbk])
                    for dc in range(NDC):
                        tb = (n * NDC + dc) % 2
                        ak, uk2 = f"d_dA{tb}", f"d_dBu{tb}"
                        sok, cek = f"d_so{tb}", f"d_ce{tb}"
                        S.op("scalar", lambda e, dc=dc, n=n, tb=tb, cb_=cb_:
                             e.activation(
                                 out=dA_[tb][:], in_=delta[dc][cb_][:],
                                 func=AF.Exp,
                                 scale=negA_t[:, dc * DS + n:dc * DS + n + 1]),
                            ins=[f"d_del{dc}_{cb_}", "negA"], outs=[ak])
                        S.op("vector", lambda e, dc=dc, tb=tb, nb=nb, cb_=cb_:
                             e.tensor_tensor(
                                 out=dBu_[tb][:], in0=du_b[dc][cb_][:],
                                 in1=bbc[nb][:], op=ALU.mult),
                             ins=[f"d_du{dc}_{cb_}", bbk], outs=[uk2])
                        S.op("vector", lambda e, dc=dc, n=n, tb=tb:
                             e.tensor_tensor_scan(
                                 out=so_[tb][:], data0=dA_[tb][:],
                                 data1=dBu_[tb][:],
                                 initial=state_t[:, dc * DS + n:
                                                 dc * DS + n + 1],
                                 op0=ALU.mult, op1=ALU.add),
                             ins=[ak, uk2, "state"], outs=[sok])
                        S.op("vector", lambda e, dc=dc, n=n, tb=tb:
                             e.tensor_copy(
                                 out=state_t[:, dc * DS + n:dc * DS + n + 1],
                                 in_=so_[tb][:, LC - 1:LC]),
                             ins=[sok], outs=["state"])
                        S.op("vector", lambda e, tb=tb, nb=nb: e.tensor_tensor(
                            out=so_[tb][:], in0=so_[tb][:], in1=cbc[nb][:],
                            op=ALU.mult), ins=[sok, cbk], outs=[sok])
                        if n == 0:
                            S.op("vector", lambda e, dc=dc, tb=tb:
                                 e.tensor_copy(
                                     out=yacc[dc][:], in_=so_[tb][:]),
                                 ins=[sok], outs=[f"d_y{dc}"])
                        else:
                            S.op("vector", lambda e, dc=dc, tb=tb:
                                 e.tensor_tensor(
                                     out=yacc[dc][:], in0=yacc[dc][:],
                                     in1=so_[tb][:], op=ALU.add),
                                 ins=[sok], outs=[f"d_y{dc}"])
                blk = g0 // TOKC
                colg = g0 % TOKC
                for dc in range(NDC):
                    ygb = so_[0]
                    S.op("vector", lambda e, dc=dc, cb_=cb_, ygb=ygb:
                         e.tensor_scalar(
                             out=ygb[:], in0=u_b[dc][cb_][:],
                             scalar1=D_t[:, dc:dc + 1], scalar2=None,
                             op0=ALU.mult),
                        ins=[f"p1u{dc}_{cb_}", "Dt"], outs=["d_so0"])
                    S.op("vector", lambda e, dc=dc, ygb=ygb: e.tensor_tensor(
                        out=ygb[:], in0=ygb[:], in1=yacc[dc][:], op=ALU.add),
                        ins=[f"d_y{dc}"], outs=["d_so0"])
                    S.op("vector", lambda e, dc=dc, cb_=cb_, ygb=ygb:
                         e.tensor_tensor(
                             out=ygb[:], in0=ygb[:], in1=sz_b[dc][cb_][:],
                             op=ALU.mult),
                        ins=[f"p1sz{dc}_{cb_}"], outs=["d_so0"])
                    S.dma("sync",
                          yg_a2ai[blk * DCORE + dc * 128:
                                  blk * DCORE + (dc + 1) * 128,
                                  colg:colg + LC],
                          ygb[:], ins=["d_so0"], outs=["dram_ygi"], sem="dst")

            NFU = NFH * HC

            def emit_ffn(u):
                fh = u // HC
                h_c = u % HC
                if h_c == 0:
                    qq = (fh * FHT) // QTOK
                    qc = (fh * FHT) % QTOK
                    xv = bass.AP(xn_agi.tensor, qq * DM * QTOK + qc,
                                 [[QTOK, 128], [128 * QTOK, KC], [1, FHT]])
                    S.dma("sync", xnf[:], xv,
                          ins=[f"dram_xnagi{qq}"], outs=["fxn"], sem="dx")
                wb = h_c % 2
                wk = f"fw{wb}"
                a_r = h_c % NC_
                s_s = h_c // NC_
                S.dma("sync", w12[wb][:, 0:KC * 128],
                      w1_ago[a_r * 128:(a_r + 1) * 128,
                             s_s * KC * 128:(s_s + 1) * KC * 128],
                      ins=["dram_wago0"], outs=[wk], sem="dw")
                S.dma("sync", w12[wb][:, KC * 128:2 * KC * 128],
                      w2_ago[a_r * 128:(a_r + 1) * 128,
                             s_s * KC * 128:(s_s + 1) * KC * 128],
                      ins=["dram_wago1"], outs=[wk], sem="dw")
                for f in range(NFF):
                    fb = f % 2
                    psx = psF1 if fb == 0 else psF2
                    S.op("tensor", lambda e, wb=wb, f=f, psx=psx: [
                        e.matmul(out=psx[:],
                                 lhsT=w12[wb][:, k * 128:(k + 1) * 128],
                                 rhs=xnf[:, k * FHT + f * PSF:
                                         k * FHT + (f + 1) * PSF],
                                 start=(k == 0), stop=(k == KC - 1))
                        for k in range(KC)][-1],
                        ins=[wk, "fxn"],
                        outs=[f"fps{fb}"], sem="pe")
                    S.op("scalar", lambda e, fb=fb, psx=psx: e.activation(
                        out=tb_[fb][:], in_=psx[:], func=AF.Tanh,
                        scale=0.5), ins=[f"fps{fb}"], outs=[f"ftb{fb}"])
                    S.op("tensor", lambda e, wb=wb, f=f: [
                        e.matmul(out=psF3[:],
                                 lhsT=w12[wb][:, (KC + k) * 128:
                                              (KC + k + 1) * 128],
                                 rhs=xnf[:, k * FHT + f * PSF:
                                         k * FHT + (f + 1) * PSF],
                                 start=(k == 0), stop=(k == KC - 1))
                        for k in range(KC)][-1],
                        ins=[wk, "fxn"],
                        outs=["fps2"], sem="pe")
                    S.op("scalar", lambda e, fb=fb: e.activation(
                        out=x2h[fb][:], in_=psF3[:], func=AF.Identity,
                        scale=0.5), ins=["fps2"], outs=[f"fx2h{fb}"])
                    S.op("gpsimd", lambda e, fb=fb: e.tensor_scalar(
                        out=hb_[fb][:], in0=tb_[fb][:], scalar1=1.0,
                        scalar2=None, op0=ALU.add),
                        ins=[f"ftb{fb}"], outs=[f"fhb{fb}"])
                    S.op("gpsimd", lambda e, fb=fb: e.tensor_tensor(
                        out=hb_[fb][:], in0=hb_[fb][:], in1=x2h[fb][:],
                        op=ALU.mult),
                        ins=[f"fx2h{fb}"], outs=[f"fhb{fb}"])
                    S.dma("sync",
                          hid_dram[h_c * 128:(h_c + 1) * 128,
                                   fh * FHT + f * PSF:fh * FHT + (f + 1) * PSF],
                          hb_[fb][:], ins=[f"fhb{fb}"], outs=["dram_hid"],
                          sem="dst")

            emit_p1_start(0)
            for m in range(2 * NDC):
                emit_p1_m(0, m)
            emit_p1_conv(0)
            emit_d_load(0)
            fu = 0
            SKIP0 = min(3, NCH - 1)
            per = ceil_div(NFU, max(1, NCH - 1 - SKIP0))
            emit_d_head(0)
            for c in range(NCH):
                units = []
                if c + 1 < NCH:
                    emit_p1_start(c + 1)
                    for m in range(2 * NDC):
                        units.append(lambda m=m: emit_p1_m(c + 1, m))
                    units.append(lambda: emit_p1_conv(c + 1))
                    units.append(lambda: emit_d_load(c + 1))
                    units.append(lambda: emit_d_head(c + 1))
                stop = min(NFU, max(0, (c + 1 - SKIP0)) * per)
                while fu < stop:
                    units.append(lambda fu=fu: emit_ffn(fu))
                    fu += 1

                def inject(n, units=units):
                    k0 = len(units) * n // DS
                    k1 = len(units) * (n + 1) // DS
                    for k in range(k0, k1):
                        units[k]()
                emit_d_scan(c, inject)
            while fu < NFU:
                emit_ffn(fu)
                fu += 1

            with allow_ici_all_to_all():
                a2a_tok = S.op("gpsimd", lambda e: e.collective_compute(
                    "AllToAll", ALU.bypass, replica_groups=rg,
                    ins=[yg_a2ai.opt()], outs=[yg_a2ao.opt()]),
                    ins=["dram_ygi"], outs=["dram_ygo"], sem="cc",
                    extra=[S.current("dst")])

        S.fence()
        with contextlib.ExitStack() as ph:
            def sbW(name, shape, dt):
                return ph.enter_context(nc.sbuf_tensor(name, shape, dt))
            psum_ = mk_psum(ph)
            NBK = NC_ * NSLOT
            w3t = [sbW(f"f_w3{i}", [128, NBK * 128], BF16) for i in range(2)]
            x3b = [sbW(f"f_x3{i}", [128, PSF], F32) for i in range(2)]
            ps_f = [psum_(f"w3_ps{i}", [128, PSF], F32) for i in range(2)]

            def blk_of(kh):
                return (kh % NC_) * NSLOT + kh // NC_
            hid_all = sbW("f_hall", [128, HC * PSF], BF16)
            for fq in range(TOKC // PSF):
                hv = bass.AP(hid_dram.tensor, fq * PSF,
                             [[TOKC, 128], [128 * TOKC, HC], [1, PSF]])
                S.dma("sync", hid_all[:], hv, ins=["dram_hid"],
                      outs=["fhall"], sem="dx")
                for m in range(MDM):
                    wb = m % 2
                    wk = f"fw3{wb}"
                    w3v = bass.AP(w3_ago.tensor, m * NSLOT * 128,
                                  [[L3R, 128], [128 * L3R, NC_],
                                   [1, NSLOT * 128]])
                    S.dma("sync", w3t[wb][:, 0:NBK * 128], w3v,
                          ins=["dram_wago2"], outs=[wk], sem="dw")
                    pb = m % 2
                    S.op("tensor", lambda e, wb=wb, pb=pb: [
                        e.matmul(out=ps_f[pb][:],
                                 lhsT=w3t[wb][:, blk_of(kh) * 128:
                                              blk_of(kh) * 128 + 128],
                                 rhs=hid_all[:, kh * PSF:(kh + 1) * PSF],
                                 start=(kh == 0),
                                 stop=(kh == HC - 1)) for kh in range(HC)][-1],
                        ins=[wk, "fhall"],
                        outs=[f"w3ps{pb}"], sem="pe")
                    S.op("scalar", lambda e, pb=pb: e.activation(
                        out=x3b[pb][:], in_=ps_f[pb][:], func=AF.Identity),
                        ins=[f"w3ps{pb}"], outs=[f"fx3{pb}"])
                    S.dma("sync", x3_dram[m * 128:(m + 1) * 128,
                                          fq * PSF:(fq + 1) * PSF],
                          x3b[pb][:], ins=[f"fx3{pb}"], outs=["dram_x3"],
                          sem="dst")

        S.fence()
        with contextlib.ExitStack() as ph:
            def sb7(name, shape, dt):
                return ph.enter_context(nc.sbuf_tensor(name, shape, dt))
            psum_ = mk_psum(ph)
            FQ = 1024 if TOKC % 1024 == 0 else PSF
            NQ = TOKC // FQ
            NPF = FQ // PSF
            ygt_all = sb7("s_ygall", [128, KDI * FQ], BF16)
            wot = [sb7(f"s_wo{i}", [128, KDI * 128], BF16) for i in range(2)]
            x3t = [sb7(f"s_x3{i}", [128, FQ], F32) for i in range(2)]
            smf = [sb7(f"s_smf{i}", [128, FQ], F32) for i in range(2)]
            outb = [sb7(f"s_out{t}", [128, DM], F32) for t in range(FQ // 128)]
            resb = sb7("s_res", [128, DM], F32)
            ps_o = [psum_(f"s_pso{i}", [128, PSF], F32) for i in range(2)]
            ps_t7 = [psum_(f"s_pst{i}", [128, PSF], F32) for i in range(2)]

            for fq in range(NQ):
                ygv = bass.AP(yg_a2ao.tensor, fq * FQ,
                              [[TOKC, 128], [128 * TOKC, KDI], [1, FQ]])
                S.dma("sync", ygt_all[:], ygv, ins=["dram_ygo"],
                      outs=["sygall"], extra=[a2a_tok], sem="dx")
                for m in range(MDM):
                    wb = m % 2
                    wk = f"swo{wb}"
                    wov = bass.AP(wo_ago.tensor, m * NDD * 128,
                                  [[LOR, 128], [128 * LOR, NC_], [1, NDD * 128]])
                    S.dma("sync", wot[wb][:, 0:KDI * 128], wov,
                          ins=["dram_woago"], outs=[wk], sem="dw")
                    xk3 = f"sx3{m % 2}"
                    S.dma("sync", x3t[m % 2][:],
                          x3_dram[m * 128:(m + 1) * 128, fq * FQ:(fq + 1) * FQ],
                          ins=["dram_x3"], outs=[xk3], sem="dld")
                    for f2 in range(NPF):
                        pb = f2 % 2
                        S.op("tensor", lambda e, wb=wb, f2=f2, fq=fq, pb=pb: [
                            e.matmul(out=ps_o[pb][:],
                                     lhsT=wot[wb][:, k * 128:(k + 1) * 128],
                                     rhs=ygt_all[:, k * FQ + f2 * PSF:
                                                 k * FQ + (f2 + 1) * PSF],
                                     start=(k == 0), stop=(k == KDI - 1))
                            for k in range(KDI)][-1],
                            ins=[wk, "sygall"],
                            outs=[f"spo{pb}"], sem="pe")
                        S.op("vector", lambda e, pb=pb, f2=f2, m=m:
                             e.tensor_tensor(
                                 out=smf[m % 2][:, f2 * PSF:(f2 + 1) * PSF],
                                 in0=ps_o[pb][:],
                                 in1=x3t[m % 2][:, f2 * PSF:(f2 + 1) * PSF],
                                 op=ALU.add),
                             ins=[f"spo{pb}", xk3], outs=[f"ssmf{m % 2}"])
                    for tg in range(FQ // PSF):
                        pb = tg % 2
                        pk = f"spt{pb}"
                        for j in range(PSF // 128):
                            tt = tg * (PSF // 128) + j
                            S.op("tensor", lambda e, tt=tt, j=j, pb=pb, m=m:
                                 e.transpose(
                                     out=ps_t7[pb][:, j * 128:(j + 1) * 128],
                                     in_=smf[m % 2][:, tt * 128:(tt + 1) * 128],
                                     identity=eyef_t[:]),
                                 ins=[f"ssmf{m % 2}", "eyef"], outs=[pk])
                        for j in range(PSF // 128):
                            tt = tg * (PSF // 128) + j
                            S.op("vector", lambda e, tt=tt, j=j, pb=pb, m=m:
                                 e.tensor_copy(
                                     out=outb[tt][:, m * 128:(m + 1) * 128],
                                     in_=ps_t7[pb][:, j * 128:(j + 1) * 128]),
                                 ins=[pk], outs=[f"sout{tt}"])
                for t in range(FQ // 128):
                    row0 = fq * FQ + t * 128
                    S.dma("sync", resb[:], x_tok[row0:row0 + 128, :],
                          outs=["sres"])
                    S.op("vector", lambda e, t=t: e.tensor_tensor(
                        out=outb[t][:], in0=outb[t][:], in1=resb[:], op=ALU.add),
                        ins=["sres", f"sout{t}"], outs=[f"sout{t}"])
                    S.dma("sync", out_ext[row0:row0 + 128, :], outb[t][:],
                          ins=[f"sout{t}"], outs=["dram_out"], sem="dst")

            if debug:
                for nm, (ap_out, src) in dbg.items():
                    S.dma("sync", ap_out, src, ins=[], outs=[],
                          extra=[S.current("dst"), S.current("cc"),
                                 S.current("dgp")], sem="dst")

            ftok = S.publish("dst", S.cnt["dst"])
            S.op("gpsimd", lambda e, v=ftok[1]: e.wait_ge(S.sems["dst_r"], v),
                 sem=None)

        with allow_ici_all_to_all(), nc.Block() as block:
            @block.sync
            def _(eng):
                for f in S.q["sync"]:
                    f(eng)

            @block.tensor
            def _(eng):
                for f in S.q["tensor"]:
                    f(eng)

            @block.vector
            def _(eng):
                for f in S.q["vector"]:
                    f(eng)

            @block.scalar
            def _(eng):
                for f in S.q["scalar"]:
                    f(eng)

            @block.gpsimd
            def _(eng):
                for f in S.q["gpsimd"]:
                    f(eng)

    return nc


def make_inputs(cfg, full_inputs):
    B, L, DM, DI = cfg["B"], cfg["L"], cfg["DM"], cfg["DI"]
    DS, DTR, HFF = cfg["DS"], cfg["DTR"], cfg["HFF"]
    TOK = B * L
    TOKC = TOK // NC_
    DCORE = DI // NC_
    HC = HFF // 128
    NSLOT = ceil_div(HC, NC_)
    fi = full_inputs
    x = np.ascontiguousarray(fi["hidden"].reshape(TOK, DM))
    w_in = fi["in_proj"]
    eye_b = np.eye(128, dtype=ml_dtypes.bfloat16)
    DS_ = cfg["DS"]
    sel = np.zeros((2 * DS_, 2 * DS_ * 128), dtype=ml_dtypes.bfloat16)
    for j_ in range(2 * DS_):
        sel[j_, j_ * 128:(j_ + 1) * 128] = 1
    eye_f = np.eye(128, dtype=np.float32)
    ones_b = np.ones((1, 128), dtype=ml_dtypes.bfloat16)
    ones_fr = np.ones((1, 128), dtype=np.float32)
    ones_fc = np.ones((128, 1), dtype=np.float32)
    maps = []
    for i in range(NC_):
        dsl = slice(i * DCORE, (i + 1) * DCORE)
        chunks = [c for c in range(HC) if c % NC_ == i]
        w1s = np.zeros((DM, NSLOT * 128), np.float32)
        w2s = np.zeros((DM, NSLOT * 128), np.float32)
        w3s = np.zeros((NSLOT * 128, DM), np.float32)
        for s, c in enumerate(chunks):
            w1s[:, s * 128:(s + 1) * 128] = fi["w1"][c * 128:(c + 1) * 128, :].T
            w2s[:, s * 128:(s + 1) * 128] = fi["w2"][c * 128:(c + 1) * 128, :].T
            w3s[s * 128:(s + 1) * 128, :] = fi["w3"][:, c * 128:(c + 1) * 128].T
        maps.append(dict(
            x_tok=np.ascontiguousarray(x[i * TOKC:(i + 1) * TOKC]),
            winT_x=np.ascontiguousarray(w_in[dsl].T),
            winT_z=np.ascontiguousarray(w_in[DI + i * DCORE:
                                             DI + (i + 1) * DCORE].T),
            convw=np.ascontiguousarray(fi["conv_w"][dsl, 0, :]),
            convb=np.ascontiguousarray(fi["conv_b"][dsl, None]),
            negA_in=np.ascontiguousarray(fi["negA"][dsl]),
            D_in=np.ascontiguousarray(fi["D"][dsl, None]),
            dtb_in=np.ascontiguousarray(fi["dt_b"][dsl, None]),
            dtwT=np.ascontiguousarray(fi["dt_w"][dsl].T),
            xpwT=np.ascontiguousarray(fi["xp_w"][:, dsl].T),
            woT_sl=np.ascontiguousarray(fi["out_w"][:, dsl].T),
            w1T_sl=w1s, w2T_sl=w2s, w3T_sl=w3s,
            ones_bf=ones_b, ones_fr=ones_fr, ones_fc=ones_fc,
            eye_bf=eye_b, eye_f=eye_f, sel_in=sel,
        ))
    return maps


def kernel(L0_attn_hidden_states, L0_attn_in_proj_w, L0_attn_conv_w,
           L0_attn_conv_b, L0_attn_x_proj_w, L0_attn_dt_proj_w,
           L0_attn_dt_proj_b, L0_attn_out_proj_w, L0_attn_negA, L0_attn_D,
           L0_ffn_w1, L0_ffn_w2, L0_ffn_w3):
    cfg = FULL
    fi = dict(hidden=np.asarray(L0_attn_hidden_states),
              in_proj=np.asarray(L0_attn_in_proj_w),
              conv_w=np.asarray(L0_attn_conv_w),
              conv_b=np.asarray(L0_attn_conv_b),
              xp_w=np.asarray(L0_attn_x_proj_w),
              dt_w=np.asarray(L0_attn_dt_proj_w),
              dt_b=np.asarray(L0_attn_dt_proj_b),
              out_w=np.asarray(L0_attn_out_proj_w),
              negA=np.asarray(L0_attn_negA),
              D=np.asarray(L0_attn_D),
              w1=np.asarray(L0_ffn_w1),
              w2=np.asarray(L0_ffn_w2),
              w3=np.asarray(L0_ffn_w3))
    nc = build_program(cfg)
    in_maps = make_inputs(cfg, fi)
    res = run_bass_kernel_spmd(nc, in_maps, core_ids=list(range(NC_)))
    out = np.concatenate([np.asarray(r["out_sl"]) for r in res.results], axis=0)
    B, L, DM = cfg["B"], cfg["L"], cfg["DM"]
    return out.reshape(B, L, DM).astype(np.float32)


# version 37 (final)
# speedup vs baseline: 1.0119x; 1.0044x over previous
import contextlib
import numpy as np
import ml_dtypes
import concourse.bass as bass
from concourse.aot_env import allow_ici_all_to_all
import concourse.mybir as mybir
from concourse.bass_utils import run_bass_kernel_spmd

F32 = mybir.dt.float32
BF16 = mybir.dt.bfloat16
AF = mybir.ActivationFunctionType
ALU = mybir.AluOpType
AX = mybir.AxisListType

NC_ = 8

FULL = dict(B=2, L=16384, DM=2048, DI=4096, DS=16, DTR=128, HFF=5632, LC=1024,
            PSF=512)
MINI = dict(B=2, L=1024, DM=256, DI=2048, DS=16, DTR=128, HFF=1280, LC=256,
            PSF=128)


class Buf:
    __slots__ = ("writer", "readers")

    def __init__(self):
        self.writer = None
        self.readers = []


class Sched:
    ENGINES = ("sync", "tensor", "vector", "scalar", "gpsimd")
    DMA_SEMS = {"dld": "sync", "dst": "gpsimd", "dgp": "gpsimd", "cc": "gpsimd",
                "dx": "sync", "dw": "sync"}

    def __init__(self, nc, stack):
        self.nc = nc
        self.q = {e: [] for e in self.ENGINES}
        self.sems = {}
        self.cnt = {}
        self.memo = {e: {} for e in self.ENGINES}
        self.bufs = {}
        self.pub = {}
        for s in ("pe", "dve", "act", "pool", "dld", "dst", "dgp", "cc",
                  "dx", "dw", "dld_r", "dst_r", "dgp_r", "cc_r", "dx_r", "dw_r"):
            self.sems[s] = stack.enter_context(nc.semaphore(s))
            self.cnt[s] = 0
        for s in self.DMA_SEMS:
            self.pub[s] = (0, 0)

    def publish(self, s, upto):
        cov, rv = self.pub[s]
        if cov >= upto:
            return (s + "_r", rv)
        eng = self.DMA_SEMS[s]
        tot = self.cnt[s]
        self._wait(eng, s, tot)
        rs = s + "_r"
        self.cnt[rs] += 1
        h = self.sems[rs]
        v = self.cnt[rs]
        self.q[eng].append(lambda e, h=h: e.sem_inc(h, 1))
        self.pub[s] = (tot, v)
        return (rs, v)

    def buf(self, key):
        if key not in self.bufs:
            self.bufs[key] = Buf()
        return self.bufs[key]

    def _wait(self, eng, sem, val):
        if val <= 0 or self.memo[eng].get(sem, 0) >= val:
            return
        self.memo[eng][sem] = val
        h = self.sems[sem]
        self.q[eng].append(lambda e, h=h, v=val: e.wait_ge(h, v))

    def op(self, eng, build, ins=(), outs=(), sem="", amt=1, extra=()):
        if sem == "":
            sem = {"tensor": "pe", "vector": "dve", "scalar": "act",
                   "gpsimd": "pool"}.get(eng)
        deps = {}

        def add(d):
            if d is None:
                return
            if d[0] in self.DMA_SEMS:
                d = self.publish(d[0], d[1])
            if deps.get(d[0], 0) < d[1]:
                deps[d[0]] = d[1]
        for k in ins:
            add(self.buf(k).writer)
        for k in outs:
            b = self.buf(k)
            add(b.writer)
            for r in b.readers:
                add(r)
        for d in extra:
            add(d)
        if eng == "tensor":
            deps.pop("pe", None)
        for s, v in deps.items():
            self._wait(eng, s, v)
        h = self.sems[sem] if sem else None
        if h is not None:
            self.cnt[sem] += amt
            self.q[eng].append(lambda e, b=build, h=h, a=amt: b(e).then_inc(h, a))
            tok = (sem, self.cnt[sem])
        else:
            self.q[eng].append(lambda e, b=build: b(e))
            tok = None
        for k in outs:
            b = self.buf(k)
            b.writer = tok
            b.readers = []
        if tok:
            for k in ins:
                self.buf(k).readers.append(tok)
        return tok

    def dma(self, eng, out_ap, in_ap, ins=(), outs=(), sem="", extra=(), **kw):
        if sem == "":
            sem = {"sync": "dld", "gpsimd": "dgp"}.get(eng, "dst")
        eng = self.DMA_SEMS[sem]
        return self.op(eng, lambda e: e.dma_start(out=out_ap, in_=in_ap, **kw),
                       ins=ins, outs=outs, sem=sem, amt=16, extra=extra)

    def current(self, sem):
        return (sem, self.cnt[sem])

    def fence(self, include_cc=False):
        toks = [(sm, self.cnt[sm]) for sm in ("pe", "dve", "act", "pool")]
        for sm in self.DMA_SEMS:
            if sm == "cc" and not include_cc:
                continue
            toks.append(self.publish(sm, self.cnt[sm]))
        for eng in self.ENGINES:
            for (sm, v) in toks:
                self._wait(eng, sm, v)


def ceil_div(a, b):
    return -(-a // b)


def build_program(cfg, debug=False):
    B, L, DM, DI = cfg["B"], cfg["L"], cfg["DM"], cfg["DI"]
    DS, DTR, HFF, LC, PSF = cfg["DS"], cfg["DTR"], cfg["HFF"], cfg["LC"], cfg["PSF"]
    TOK = B * L
    TOKC = TOK // NC_
    DCORE = DI // NC_
    KC = DM // 128
    NDC = DCORE // 128
    HC = HFF // 128
    NSLOT = ceil_div(HC, NC_)
    KDI = DI // 128
    NCH = TOK // LC
    NF = LC // PSF
    MDM = DM // 128
    TTOK = TOKC // 128
    NXP = DTR + 2 * DS
    assert LC % PSF == 0 and TOKC % LC == 0 and L % LC == 0 and DTR == 128

    nc = bass.Bass("TRN2", target_bir_lowering=False, debug=False)

    def din(name, shape, dt=F32):
        return nc.dram_tensor(name, shape, dt, kind="ExternalInput").ap()

    def dout(name, shape, dt=F32):
        return nc.dram_tensor(name, shape, dt, kind="ExternalOutput").ap()

    def dint(name, shape, dt):
        return nc.dram_tensor(name, shape, dt, kind="Internal").ap()

    x_tok = din("x_tok", [TOKC, DM])
    winT_x = din("winT_x", [DM, DCORE])
    winT_z = din("winT_z", [DM, DCORE])
    convw = din("convw", [DCORE, 4])
    convb = din("convb", [DCORE, 1])
    negA_in = din("negA_in", [DCORE, DS])
    D_in = din("D_in", [DCORE, 1])
    dtb_in = din("dtb_in", [DCORE, 1])
    dtwT = din("dtwT", [DTR, DCORE])
    xpwT = din("xpwT", [DCORE, NXP])
    woT_sl = din("woT_sl", [DCORE, DM])
    w1T_sl = din("w1T_sl", [DM, NSLOT * 128])
    w2T_sl = din("w2T_sl", [DM, NSLOT * 128])
    w3T_sl = din("w3T_sl", [NSLOT * 128, DM])
    ones_bf = din("ones_bf", [1, 128], BF16)
    ones_fr = din("ones_fr", [1, 128])
    ones_fc = din("ones_fc", [128, 1])
    eye_bf = din("eye_bf", [128, 128], BF16)
    sel_in = din("sel_in", [2 * DS, 2 * DS * 128], BF16)
    eye_f = din("eye_f", [128, 128])
    out_ext = dout("out_sl", [TOKC, DM])

    NQX = max(1, TOKC // max(LC, PSF * 2, TOKC // 4))
    QTOK = TOKC // NQX
    xn_agi = dint("xn_agi", [NQX * DM, QTOK], BF16)
    xn_ago = dint("xn_ago", [NC_ * NQX * DM, QTOK], BF16)
    gp_ari = dint("gp_ari", [1, 4], F32)
    gp_aro = dint("gp_aro", [1, 4], F32)
    NDD = DCORE // 128
    L1R = NSLOT * KC * 128
    L3R = MDM * NSLOT * 128
    LOR = MDM * NDD * 128
    w1_agi = dint("w1_agi", [128, L1R], BF16)
    w1_ago = dint("w1_ago", [NC_ * 128, L1R], BF16)
    w2_agi = dint("w2_agi", [128, L1R], BF16)
    w2_ago = dint("w2_ago", [NC_ * 128, L1R], BF16)
    w3_agi = dint("w3_agi", [128, L3R], BF16)
    w3_ago = dint("w3_ago", [NC_ * 128, L3R], BF16)
    wo_agi = dint("wo_agi", [128, LOR], BF16)
    wo_ago = dint("wo_ago", [NC_ * 128, LOR], BF16)
    LW = 2 * NDC * KC * 128
    win_img = dint("win_img", [128, LW], BF16)
    xdbl_p = dint("xdbl_p", [NCH * NXP, LC], BF16)
    xdbl_f = dint("xdbl_f", [NCH * NXP, LC], BF16)
    hid_dram = dint("hid_dram", [HFF, TOKC], BF16)
    x3_dram = dint("x3_dram", [DM, TOKC], F32)
    yg_a2ai = dint("yg_a2ai", [NC_ * DCORE, TOKC], BF16)
    yg_a2ao = dint("yg_a2ao", [NC_ * DCORE, TOKC], BF16)

    if debug:
        dbg = {}
        for nm, src, shape, dt in [
                ("dbg_xn", xn_agi, [NQX * DM, QTOK], BF16),
                ("dbg_xdbl", xdbl_f, [NCH * NXP, LC], BF16),
                ("dbg_yg", yg_a2ai, [NC_ * DCORE, TOKC], BF16),
                ("dbg_hid", hid_dram, [HFF, TOKC], BF16),
                ("dbg_x3", x3_dram, [DM, TOKC], F32),
                ("dbg_w1q", w1_ago, [NC_ * 128, NSLOT * KC * 128], BF16)]:
            dbg[nm] = (dout(nm, shape, dt), src)

    rg = [list(range(NC_))]
    ar_tok = {}

    with contextlib.ExitStack() as stack:
        S = Sched(nc, stack)

        def sb(name, shape, dt):
            return stack.enter_context(nc.sbuf_tensor(name, shape, dt))

        def mk_psum(st):
            def f(name, shape, dt=F32):
                return st.enter_context(nc.psum_tensor(name, shape, dt))
            return f

        eyeb_t = sb("eyeb_t", [128, 128], BF16)
        sel_t = sb("sel_t", [2 * DS, 2 * DS * 128], BF16)
        eyef_t = sb("eyef_t", [128, 128], F32)
        onesb_t = sb("onesb_t", [1, 128], BF16)
        onesfr_t = sb("onesfr_t", [1, 128], F32)
        onesfc_t = sb("onesfc_t", [128, 1], F32)
        negA_t = sb("negA_t", [128, NDC * DS], F32)
        convw_t = sb("convw_t", [128, NDC * 4], F32)
        convb_t = sb("convb_t", [128, NDC], F32)
        D_t = sb("D_t", [128, NDC], F32)
        dtb_t = sb("dtb_t", [128, NDC], F32)
        dtw_t = sb("dtw_t", [128, DCORE], BF16)
        xpw_t = sb("xpw_t", [128, NDC * NXP], BF16)
        state_t = sb("state_t", [128, NDC * DS], F32)
        ginv_t = sb("ginv_t", [128, 4], F32)

        S.dma("sync", eyeb_t[:], eye_bf, outs=["eyeb"])
        S.dma("sync", sel_t[:], sel_in, outs=["selt"])
        S.dma("sync", eyef_t[:], eye_f, outs=["eyef"])
        S.dma("sync", onesb_t[:], ones_bf, outs=["onesb"])
        S.dma("sync", onesfr_t[:], ones_fr, outs=["onesfr"])
        S.dma("sync", onesfc_t[:], ones_fc, outs=["onesfc"])
        for dc in range(NDC):
            S.dma("sync", negA_t[:, dc * DS:(dc + 1) * DS],
                  negA_in[dc * 128:(dc + 1) * 128, :], outs=["negA"])
            S.dma("sync", convw_t[:, dc * 4:(dc + 1) * 4],
                  convw[dc * 128:(dc + 1) * 128, :], outs=["convw"])
            S.dma("sync", convb_t[:, dc:dc + 1],
                  convb[dc * 128:(dc + 1) * 128, :], outs=["convb"])
            S.dma("sync", D_t[:, dc:dc + 1],
                  D_in[dc * 128:(dc + 1) * 128, :], outs=["Dt"])
            S.dma("sync", dtb_t[:, dc:dc + 1],
                  dtb_in[dc * 128:(dc + 1) * 128, :], outs=["dtb"])

        with contextlib.ExitStack() as ph:
            def sb0(name, shape, dt):
                return ph.enter_context(nc.sbuf_tensor(name, shape, dt))
            psum_ = mk_psum(ph)
            xt = [sb0(f"p0_xt{i}", [128, DM], F32) for i in range(4)]
            xnb = [sb0(f"p0_xnb{i}", [128, DM], BF16) for i in range(4)]
            red2 = [sb0(f"p0_red{i}", [128, 8], F32) for i in range(2)]
            sq2 = [sb0(f"p0_sq{i}", [128, DM], F32) for i in range(2)]
            evb = [sb0(f"p0_evb{i}", [128, PSF], BF16) for i in range(2)]
            ps_tr = [psum_(f"p0_pstr{i}", [128, PSF], BF16) for i in range(2)]
            WMAX = max(DM, NSLOT * 128)
            wt2 = [sb0(f"p0_wt{i}", [128, WMAX], F32) for i in range(4)]
            WSEM = ["dld", "dgp", "dld", "dgp"]
            wq_bf = sb0("p0_wqb", [128, WMAX], BF16)
            gpos = sb0("p0_gpos", [128, WMAX], F32)
            TRP = PSF // 128

            def xload(t):
                b = t % 4
                S.dma("sync", xt[b][:], x_tok[t * 128:(t + 1) * 128, :],
                      outs=[f"p0x{b}"], sem=("dld" if b % 2 == 0 else "dgp"))
            for t in range(min(3, TTOK)):
                xload(t)
            for t in range(TTOK):
                b = t % 4
                pr = t % 2
                red = red2[pr]
                sq = sq2[pr]
                rk = f"p0red{pr}"
                xk = f"p0x{b}"
                if t + 3 < TTOK:
                    xload(t + 3)
                S.op("vector", lambda e, b=b, red=red: e.tensor_reduce(
                    out=red[:, 0:1], in_=xt[b][:], axis=AX.X, op=ALU.add),
                    ins=[xk], outs=[rk])
                S.op("scalar", lambda e, b=b, red=red, sq=sq: e.activation(
                    out=sq[:], in_=xt[b][:], func=AF.Square, accum_out=red[:, 1:2]),
                    ins=[xk], outs=[f"p0sq{pr}", rk])
                S.op("vector", lambda e, red=red: e.tensor_scalar(
                    out=red[:, 2:3], in0=red[:, 0:1], scalar1=1.0 / DM,
                    scalar2=None, op0=ALU.mult), ins=[rk], outs=[rk])
                S.op("vector", lambda e, red=red: e.tensor_tensor(
                    out=red[:, 3:4], in0=red[:, 2:3], in1=red[:, 2:3], op=ALU.mult),
                    ins=[rk], outs=[rk])
                S.op("vector", lambda e, red=red: e.tensor_scalar(
                    out=red[:, 4:5], in0=red[:, 1:2], scalar1=1.0 / DM,
                    scalar2=None, op0=ALU.mult), ins=[rk], outs=[rk])
                S.op("vector", lambda e, red=red: e.tensor_tensor(
                    out=red[:, 5:6], in0=red[:, 4:5], in1=red[:, 3:4],
                    op=ALU.subtract), ins=[rk], outs=[rk])
                S.op("vector", lambda e, red=red: e.tensor_scalar(
                    out=red[:, 5:6], in0=red[:, 5:6], scalar1=1e-5, scalar2=None,
                    op0=ALU.add), ins=[rk], outs=[rk])
                S.op("vector", lambda e, red=red: e.reciprocal(
                    out=red[:, 6:7], in_=red[:, 5:6]), ins=[rk], outs=[rk])
                S.op("scalar", lambda e, red=red: e.activation(
                    out=red[:, 7:8], in_=red[:, 6:7], func=AF.Sqrt),
                    ins=[rk], outs=[rk])
                xnk = f"p0xn{t % 4}"
                S.op("vector", lambda e, b=b, t=t, red=red: e.tensor_scalar(
                    out=xnb[t % 4][:], in0=xt[b][:], scalar1=red[:, 2:3],
                    scalar2=red[:, 7:8], op0=ALU.subtract, op1=ALU.mult),
                    ins=[xk, rk], outs=[xnk])
                if t % TRP == TRP - 1:
                    g = t // TRP
                    pb = g % 2
                    pk = f"p0ps{pb}"
                    ek = f"p0ev{pb}"
                    for d_c in range(KC):
                        for j in range(TRP):
                            tt = g * TRP + j
                            S.op("tensor", lambda e, d_c=d_c, j=j, tt=tt, pb=pb:
                                 e.transpose(
                                     out=ps_tr[pb][:, j * 128:(j + 1) * 128],
                                     in_=xnb[tt % 4][:, d_c * 128:(d_c + 1) * 128],
                                     identity=eyeb_t[:]),
                                 ins=[f"p0xn{tt % 4}", "eyeb"], outs=[pk])
                        S.op("scalar", lambda e, pb=pb: e.activation(
                            out=evb[pb][:], in_=ps_tr[pb][:], func=AF.Identity),
                            ins=[pk], outs=[ek])
                        qq = (g * PSF) // QTOK
                        qc = (g * PSF) % QTOK
                        S.dma("sync", xn_agi[qq * DM + d_c * 128:
                                             qq * DM + (d_c + 1) * 128,
                                             qc:qc + PSF],
                              evb[pb][:], ins=[ek], outs=[f"dram_xnagi{qq}"],
                              sem="dst")
                    if ((g + 1) * PSF) % QTOK == 0:
                        qqd = (g * PSF) // QTOK
                        S.op("gpsimd", lambda e, qq=qqd: e.collective_compute(
                            "AllGather", ALU.bypass, replica_groups=rg,
                            ins=[xn_agi[qq * DM:(qq + 1) * DM, :].opt()],
                            outs=[xn_ago[qq * NC_ * DM:
                                         (qq + 1) * NC_ * DM, :].opt()]),
                            ins=[f"dram_xnagi{qqd}"], outs=[f"dram_xnago{qqd}"],
                            sem="cc", extra=[S.current("dst")])
            for half, wsrc in ((0, winT_x), (1, winT_z)):
                for kc in range(KC):
                    wb_ = kc % 2
                    S.dma("sync", wt2[wb_][:, 0:DCORE],
                          wsrc[kc * 128:(kc + 1) * 128, :],
                          outs=[f"p0wt{wb_}"], sem=("dld" if wb_ == 0 else "dx"))
                    S.op("vector", lambda e, wb_=wb_: e.tensor_copy(
                        out=wq_bf[:, 0:DCORE], in_=wt2[wb_][:, 0:DCORE]),
                        ins=[f"p0wt{wb_}"], outs=["p0wqb"])
                    out_ap = bass.AP(win_img.tensor,
                                     (half * NDC * KC + kc) * 128,
                                     [[LW, 128], [KC * 128, NDC], [1, 128]])
                    S.dma("sync", out_ap, wq_bf[:, 0:DCORE],
                          ins=["p0wqb"], outs=["dram_winimg"], sem="dst")
            S.dma("sync", wt2[0][:, 0:DCORE], dtwT, outs=["p0wt0"])
            S.op("vector", lambda e: e.tensor_copy(
                out=dtw_t[:], in_=wt2[0][:, 0:DCORE]), ins=["p0wt0"], outs=["dtw"])
            for dc in range(NDC):
                wb_ = dc % 2
                S.dma("sync", wt2[wb_][:, 0:NXP], xpwT[dc * 128:(dc + 1) * 128, :],
                      outs=[f"p0wt{wb_}"], sem=("dld" if wb_ == 0 else "dx"))
                S.op("vector", lambda e, dc=dc, wb_=wb_: e.tensor_copy(
                    out=xpw_t[:, dc * NXP:(dc + 1) * NXP],
                    in_=wt2[wb_][:, 0:NXP]),
                    ins=[f"p0wt{wb_}"], outs=["xpw"])

            wabs = sb0("p0_wabs", [128, WMAX], F32)
            acc3 = sb0("p0_acc3", [128, 4], F32)
            accr = sb0("p0_accr", [128, 1], F32)
            ps_g = psum_("p0_psg", [4, 1], F32)
            gp_sb = sb0("p0_gp", [1, 4], F32)
            gpc_sb = sb0("p0_gpc", [4, 1], F32)
            ps_gb = psum_("p0_psgb", [128, 4], F32)
            S.op("vector", lambda e: e.memset(acc3[:], 0.0), outs=["p0acc3"])
            wsl = [(w1T_sl, DM, NSLOT * 128, 0, w1_agi),
                   (w2T_sl, DM, NSLOT * 128, 1, w2_agi),
                   (w3T_sl, NSLOT * 128, DM, 2, w3_agi)]
            wtiles = []
            for (wsrc, rows, cols, j, _agi) in wsl:
                for r in range(rows // 128):
                    wtiles.append((wsrc, r, cols, j))

            def wload(i):
                wsrc, r, cols, j = wtiles[i]
                wb_ = i % 4
                S.dma("sync", wt2[wb_][:, 0:cols],
                      wsrc[r * 128:(r + 1) * 128, :],
                      outs=[f"p0wt{wb_}"], sem=WSEM[wb_])
            for i in range(min(3, len(wtiles))):
                wload(i)
            for i in range(len(wtiles)):
                wsrc, r, cols, j = wtiles[i]
                wb_ = i % 4
                S.op("scalar", lambda e, cols=cols, wb_=wb_: e.activation(
                    out=wabs[:, 0:cols], in_=wt2[wb_][:, 0:cols], func=AF.Abs,
                    accum_out=accr[:]), ins=[f"p0wt{wb_}"],
                    outs=["p0wabs", "p0accr"])
                S.op("vector", lambda e, j=j: e.tensor_tensor(
                    out=acc3[:, j:j + 1], in0=acc3[:, j:j + 1], in1=accr[:],
                    op=ALU.add), ins=["p0accr"], outs=["p0acc3"])
                if i + 3 < len(wtiles):
                    wload(i + 3)
            S.op("tensor", lambda e: e.matmul(
                out=ps_g[:], lhsT=acc3[:, 0:4], rhs=onesfc_t[:], start=True,
                stop=True), ins=["p0acc3", "onesfc"], outs=["p0psg"])
            S.op("vector", lambda e: e.tensor_copy(out=gpc_sb[:], in_=ps_g[:]),
                 ins=["p0psg"], outs=["p0gpc"])
            S.dma("sync", gp_ari.rearrange("a b -> b a"), gpc_sb[:],
                  ins=["p0gpc"], outs=["dram_gpi"], sem="dst")
            S.op("gpsimd", lambda e: e.collective_compute(
                "AllReduce", ALU.add, replica_groups=rg,
                ins=[gp_ari.opt()], outs=[gp_aro.opt()]),
                ins=["dram_gpi"], outs=["dram_gpo"], sem="cc",
                extra=[S.current("dst")])
            NW = float(DM * HFF)
            S.dma("sync", gp_sb[:], gp_aro, ins=["dram_gpo"], outs=["p0gp"])
            S.op("vector", lambda e: e.tensor_scalar(
                out=gp_sb[:], in0=gp_sb[:], scalar1=1.0 / NW, scalar2=1e-8,
                op0=ALU.mult, op1=ALU.add), ins=["p0gp"], outs=["p0gp"])
            S.op("vector", lambda e: e.reciprocal(out=gp_sb[:], in_=gp_sb[:]),
                 ins=["p0gp"], outs=["p0gp"])
            S.op("tensor", lambda e: e.matmul(
                out=ps_gb[:], lhsT=onesfr_t[:], rhs=gp_sb[:], start=True,
                stop=True), ins=["p0gp", "onesfr"], outs=["p0psgb"])
            S.op("vector", lambda e: e.tensor_copy(out=ginv_t[:], in_=ps_gb[:]),
                 ins=["p0psgb"], outs=["ginv"])

            qtiles = []
            for (wsrc, rows, cols, j, agi) in wsl:
                for r in range(rows // 128):
                    qtiles.append((wsrc, r, cols, j, agi))

            def qload(i):
                wsrc, r, cols, j, agi = qtiles[i]
                wb_ = i % 4
                S.dma("sync", wt2[wb_][:, 0:cols],
                      wsrc[r * 128:(r + 1) * 128, :],
                      outs=[f"p0wt{wb_}"], sem=WSEM[wb_])
            for i in range(min(3, len(qtiles))):
                qload(i)
            for i in range(len(qtiles)):
                wsrc, r, cols, j, agi = qtiles[i]
                wb_ = i % 4
                if True:
                    S.op("vector", lambda e, cols=cols, j=j, wb_=wb_:
                         e.tensor_scalar(
                             out=wabs[:, 0:cols], in0=wt2[wb_][:, 0:cols],
                             scalar1=ginv_t[:, j:j + 1], scalar2=None,
                             op0=ALU.mult),
                        ins=[f"p0wt{wb_}", "ginv"], outs=["p0wabs"])
                    S.op("vector", lambda e, cols=cols: e.tensor_scalar(
                        out=gpos[:, 0:cols], in0=wabs[:, 0:cols], scalar1=0.5,
                        scalar2=None, op0=ALU.is_gt), ins=["p0wabs"],
                        outs=["p0gpos"])
                    S.op("vector", lambda e, cols=cols: e.tensor_scalar(
                        out=wabs[:, 0:cols], in0=wabs[:, 0:cols], scalar1=-0.5,
                        scalar2=None, op0=ALU.is_lt), ins=["p0wabs"],
                        outs=["p0wabs"])
                    S.op("vector", lambda e, cols=cols: e.tensor_tensor(
                        out=wq_bf[:, 0:cols], in0=gpos[:, 0:cols],
                        in1=wabs[:, 0:cols], op=ALU.subtract),
                        ins=["p0gpos", "p0wabs"], outs=["p0wqb"])
                    if j < 2:
                        out_ap = bass.AP(agi.tensor, r * 128,
                                         [[L1R, 128], [KC * 128, NSLOT], [1, 128]])
                        S.dma("sync", out_ap, wq_bf[:, 0:NSLOT * 128],
                              ins=["p0wqb"], outs=[f"dram_wagi{j}"], sem="dst")
                    else:
                        out_ap = bass.AP(agi.tensor, r * 128,
                                         [[L3R, 128], [NSLOT * 128, MDM], [1, 128]])
                        S.dma("sync", out_ap, wq_bf[:, 0:DM],
                              ins=["p0wqb"], outs=[f"dram_wagi{j}"], sem="dst")
                    if i + 3 < len(qtiles):
                        qload(i + 3)
            for j, (agi, ago) in enumerate([(w1_agi, w1_ago), (w2_agi, w2_ago),
                                            (w3_agi, w3_ago)]):
                S.op("gpsimd", lambda e, agi=agi, ago=ago: e.collective_compute(
                    "AllGather", ALU.bypass, replica_groups=rg,
                    ins=[agi.opt()], outs=[ago.opt()]),
                    ins=[f"dram_wagi{j}"], outs=[f"dram_wago{j}"], sem="cc",
                    extra=[S.current("dst")])

            for r in range(DCORE // 128):
                wb_ = r % 2
                S.dma("sync", wt2[wb_][:, 0:DM], woT_sl[r * 128:(r + 1) * 128, :],
                      outs=[f"p0wt{wb_}"], sem=("dld" if wb_ == 0 else "dx"))
                S.op("vector", lambda e, wb_=wb_: e.tensor_copy(
                    out=wq_bf[:, 0:DM], in_=wt2[wb_][:, 0:DM]),
                    ins=[f"p0wt{wb_}"], outs=["p0wqb"])
                out_ap = bass.AP(wo_agi.tensor, r * 128,
                                 [[LOR, 128], [NDD * 128, MDM], [1, 128]])
                S.dma("sync", out_ap, wq_bf[:, 0:DM],
                      ins=["p0wqb"], outs=["dram_woagi"], sem="dst")
            S.op("gpsimd", lambda e: e.collective_compute(
                "AllGather", ALU.bypass, replica_groups=rg,
                ins=[wo_agi.opt()], outs=[wo_ago.opt()]),
                ins=["dram_woagi"], outs=["dram_woago"], sem="cc",
                extra=[S.current("dst")])

            S.op("vector", lambda e: e.memset(state_t[:], 0.0), outs=["state"])

        S.fence()
        with contextlib.ExitStack() as ph:
            def sbM(name, shape, dt):
                return ph.enter_context(nc.sbuf_tensor(name, shape, dt))
            psum_ = mk_psum(ph)
            HLC = LC // 2
            win2 = [sbM(f"m_win{i}", [128, KC * 128], BF16) for i in range(2)]
            xnm = [[sbM(f"m_xn{h}_{k}", [128, HLC], BF16) for k in range(KC)]
                   for h in range(2)]
            xp_ = [sbM(f"m_xp{dc}", [128, LC + 3], BF16) for dc in range(NDC)]
            hsv = sbM("m_hsv", [128, 4 * NDC], BF16)
            cv1 = sbM("m_cv1", [128, LC], BF16)
            cv2 = sbM("m_cv2", [128, LC], BF16)
            u_b = [[sbM(f"m_u{dc}_{i}", [128, LC], BF16) for i in range(2)]
                   for dc in range(NDC)]
            sz_b = [[sbM(f"m_sz{dc}_{i}", [128, LC], BF16) for i in range(2)]
                    for dc in range(NDC)]
            xdt_b = sbM("m_xdt", [128, LC], BF16)
            xbc_b = sbM("m_xbc", [32, LC], BF16)
            dt_bf = [sbM(f"d_dtbf{i}", [128, LC], BF16) for i in range(2)]
            bc32b = [sbM(f"d_bc32b{i}", [32, LC], BF16) for i in range(2)]
            bbc = [sbM(f"d_bbc{i}", [128, LC], BF16) for i in range(2)]
            cbc = [sbM(f"d_cbc{i}", [128, LC], BF16) for i in range(2)]
            delta = [[sbM(f"d_delta{dc}_{i}", [128, LC], BF16) for i in range(2)]
                     for dc in range(NDC)]
            du_b = [[sbM(f"d_du{dc}_{i}", [128, LC], BF16) for i in range(2)]
                    for dc in range(NDC)]
            yacc = [sbM(f"d_yacc{dc}", [128, LC], BF16) for dc in range(NDC)]
            sc1 = sbM("d_sc1", [128, PSF], BF16)
            dA_ = [sbM(f"d_dA{i}", [128, LC], BF16) for i in range(2)]
            dBu_ = [sbM(f"d_dBu{i}", [128, LC], BF16) for i in range(2)]
            so_ = [sbM(f"d_so{i}", [128, LC], BF16) for i in range(2)]
            FHT = max(TOKC // 8, PSF)
            NFH = TOKC // FHT
            NFF = FHT // PSF
            xnf = [sbM(f"f_xn{k}", [128, FHT], BF16) for k in range(KC)]
            w12 = [sbM(f"f_w{i}", [128, 2 * KC * 128], BF16) for i in range(2)]
            tb_ = [sbM(f"f_tb{i}", [128, PSF], BF16) for i in range(2)]
            x2h = [sbM(f"f_x2h{i}", [128, PSF], BF16) for i in range(2)]
            hb_ = [sbM(f"f_hb{i}", [128, PSF], BF16) for i in range(2)]
            psA = psum_("m_psA", [128, PSF], F32)
            psB = psum_("m_psB", [128, PSF], F32)
            psC = psum_("m_psC", [128, PSF], F32)
            psD = psum_("m_psD", [128, PSF], F32)
            psE = psum_("m_psE", [128, PSF], F32)
            psF1 = psum_("m_psF1", [128, PSF], F32)
            psF2 = psum_("m_psF2", [128, PSF], F32)
            psF3 = psum_("m_psF3", [128, PSF], F32)

            def emit_p1_start(c):
                g0 = c * LC
                r0 = g0 // TOKC
                col0 = g0 % TOKC
                qq = col0 // QTOK
                qc = col0 % QTOK
                base = (qq * NC_ + r0) * DM
                for h in range(2):
                    for k in range(KC):
                        S.dma("sync", xnm[h][k][:],
                              xn_ago[base + k * 128:base + (k + 1) * 128,
                                     qc + h * HLC:qc + (h + 1) * HLC],
                              ins=[f"dram_xnago{qq}"], outs=[f"p1xn{h}_{k}"],
                              sem="dx")

            def emit_p1_m(c, m):
                cb_ = c % 2
                half = m // NDC
                dc = m % NDC
                wmb = m % 2
                wmk = f"p1w{wmb}"
                S.dma("sync", win2[wmb][:],
                      win_img[:, m * KC * 128:(m + 1) * KC * 128],
                      ins=["dram_winimg"], outs=[wmk], sem="dw")
                for h in range(2):
                    pb = (m * 2 + h) % 2
                    psx = psA if pb == 0 else psB
                    pk = f"p1ps{pb}"
                    S.op("tensor", lambda e, m=m, h=h, wmb=wmb, psx=psx: [
                        e.matmul(out=psx[:, 0:HLC],
                                 lhsT=win2[wmb][:, k * 128:(k + 1) * 128],
                                 rhs=xnm[h][k][:],
                                 start=(k == 0), stop=(k == KC - 1))
                        for k in range(KC)][-1],
                        ins=[wmk] + [f"p1xn{h}_{k}" for k in range(KC)],
                        outs=[pk], sem="pe")
                    if half == 0:
                        colf = 3 + h * HLC
                        S.op("scalar", lambda e, dc=dc, psx=psx, colf=colf:
                             e.activation(
                                 out=xp_[dc][:, colf:colf + HLC],
                                 in_=psx[:, 0:HLC], func=AF.Identity),
                             ins=[pk], outs=[f"p1xp{dc}"])
                    else:
                        S.op("scalar", lambda e, dc=dc, h=h, psx=psx, cb_=cb_:
                             e.activation(
                                 out=sz_b[dc][cb_][:, h * HLC:(h + 1) * HLC],
                                 in_=psx[:, 0:HLC], func=AF.Silu),
                             ins=[pk], outs=[f"p1sz{dc}_{cb_}"])

            def emit_p1_conv(c):
                g0 = c * LC
                cb_ = c % 2
                batch_first = (g0 % L) == 0
                for dc in range(NDC):
                    xpk = f"p1xp{dc}"
                    if batch_first:
                        S.op("vector", lambda e, dc=dc: e.memset(
                            xp_[dc][:, 0:3], 0.0), outs=[xpk])
                    else:
                        S.op("vector", lambda e, dc=dc: e.tensor_copy(
                             out=xp_[dc][:, 0:3],
                             in_=hsv[:, dc * 4:dc * 4 + 3]),
                             ins=["p1hsv"], outs=[xpk])
                    for tap in range(4):
                        dstb = cv1 if tap == 0 else cv2
                        dk = "p1cv1" if tap == 0 else "p1cv2"
                        S.op("scalar", lambda e, dc=dc, tap=tap,
                             d=dstb: e.activation(
                                 out=d[:], in_=xp_[dc][:, tap:tap + LC],
                                 func=AF.Identity,
                                 scale=convw_t[:, dc * 4 + tap:
                                               dc * 4 + tap + 1]),
                            ins=[xpk, "convw"], outs=[dk])
                        if tap > 0:
                            S.op("vector", lambda e: e.tensor_tensor(
                                out=cv1[:], in0=cv1[:], in1=cv2[:], op=ALU.add),
                                ins=["p1cv1", "p1cv2"], outs=["p1cv1"])
                    S.op("vector", lambda e, dc=dc: e.tensor_copy(
                        out=hsv[:, dc * 4:dc * 4 + 3],
                        in_=xp_[dc][:, LC:LC + 3]), ins=[xpk], outs=["p1hsv"])
                    S.op("scalar", lambda e, dc=dc, cb_=cb_: e.activation(
                        out=u_b[dc][cb_][:], in_=cv1[:], func=AF.Silu,
                        bias=convb_t[:, dc:dc + 1]),
                        ins=["p1cv1", "convb"], outs=[f"p1u{dc}_{cb_}"])
                for f in range(NF):
                    for dc in range(NDC):
                        S.op("tensor", lambda e, dc=dc, f=f, cb_=cb_: e.matmul(
                            out=psA[:],
                            lhsT=xpw_t[:, dc * NXP:dc * NXP + 128],
                            rhs=u_b[dc][cb_][:, f * PSF:(f + 1) * PSF],
                            start=(dc == 0), stop=(dc == NDC - 1)),
                            ins=["xpw", f"p1u{dc}_{cb_}"], outs=["p1ps0"])
                        S.op("tensor", lambda e, dc=dc, f=f, cb_=cb_: e.matmul(
                            out=psB[0:32, :],
                            lhsT=xpw_t[:, dc * NXP + 128:dc * NXP + NXP],
                            rhs=u_b[dc][cb_][:, f * PSF:(f + 1) * PSF],
                            start=(dc == 0), stop=(dc == NDC - 1)),
                            ins=["xpw", f"p1u{dc}_{cb_}"], outs=["p1ps1"])
                    S.op("scalar", lambda e, f=f: e.activation(
                        out=xdt_b[:, f * PSF:(f + 1) * PSF], in_=psA[:],
                        func=AF.Identity), ins=["p1ps0"], outs=["p1xdt"])
                    S.op("scalar", lambda e, f=f: e.activation(
                        out=xbc_b[:, f * PSF:(f + 1) * PSF], in_=psB[0:32, :],
                        func=AF.Identity), ins=["p1ps1"], outs=["p1xbc"])
                S.dma("sync", xdbl_p[c * NXP:c * NXP + 128, :], xdt_b[:],
                      ins=["p1xdt"], outs=[f"dram_xdp_{c}"], sem="dst")
                S.dma("sync", xdbl_p[c * NXP + 128:(c + 1) * NXP, :], xbc_b[:],
                      ins=["p1xbc"], outs=[f"dram_xdp_{c}"], sem="dst")
                ar_tok[c] = S.op("gpsimd", lambda e, c=c: e.collective_compute(
                    "AllReduce", ALU.add, replica_groups=rg,
                    ins=[xdbl_p[c * NXP:(c + 1) * NXP, :].opt()],
                    outs=[xdbl_f[c * NXP:(c + 1) * NXP, :].opt()]),
                    ins=[f"dram_xdp_{c}"], outs=[f"dram_xdf_{c}"], sem="cc",
                    extra=[S.current("dst")])

            def emit_d_load(c):
                cb_ = c % 2
                S.dma("sync", dt_bf[cb_][:], xdbl_f[c * NXP:c * NXP + 128, :],
                      ins=[f"dram_xdf_{c}"], outs=[f"d_dtbf{cb_}"], sem="dx")
                S.dma("sync", bc32b[cb_][:],
                      xdbl_f[c * NXP + 128:(c + 1) * NXP, :],
                      ins=[f"dram_xdf_{c}"], outs=[f"d_bc32b{cb_}"], sem="dx")

            def emit_d_head(c):
                g0 = c * LC
                cb_ = c % 2
                for dc in range(NDC):
                    for f in range(NF):
                        S.op("tensor", lambda e, dc=dc, f=f, cb_=cb_: e.matmul(
                            out=psC[:],
                            lhsT=dtw_t[:, dc * 128:(dc + 1) * 128],
                            rhs=dt_bf[cb_][:, f * PSF:(f + 1) * PSF],
                            start=True, stop=True),
                            ins=["dtw", f"d_dtbf{cb_}"], outs=["d_pC"])
                        S.op("scalar", lambda e, dc=dc, f=f: e.activation(
                            out=sc1[:], in_=psC[:], func=AF.Exp,
                            bias=dtb_t[:, dc:dc + 1]),
                            ins=["d_pC", "dtb"], outs=["d_sc1"])
                        S.op("scalar", lambda e, dc=dc, f=f, cb_=cb_:
                             e.activation(
                                 out=delta[dc][cb_][:, f * PSF:(f + 1) * PSF],
                                 in_=sc1[:], func=AF.Ln, bias=1.0),
                            ins=["d_sc1"], outs=[f"d_del{dc}_{cb_}"])
                    S.op("vector", lambda e, dc=dc, cb_=cb_: e.tensor_tensor(
                        out=du_b[dc][cb_][:], in0=delta[dc][cb_][:],
                        in1=u_b[dc][cb_][:],
                        op=ALU.mult), ins=[f"d_del{dc}_{cb_}", f"p1u{dc}_{cb_}"],
                        outs=[f"d_du{dc}_{cb_}"])
                if (g0 % L) == 0:
                    S.op("vector", lambda e: e.memset(state_t[:], 0.0),
                         outs=["state"])

            def emit_d_scan(c, inject=None):
                g0 = c * LC
                cb_ = c % 2
                for n in range(DS):
                    if inject:
                        inject(n)
                    nb = n % 2
                    bbk, cbk = f"d_bbc{nb}", f"d_cbc{nb}"
                    for f in range(NF):
                        S.op("tensor", lambda e, n=n, f=f, cb_=cb_: e.matmul(
                            out=psD[:], lhsT=sel_t[:, n * 128:(n + 1) * 128],
                            rhs=bc32b[cb_][:, f * PSF:(f + 1) * PSF],
                            start=True, stop=True),
                            ins=[f"d_bc32b{cb_}", "selt"], outs=["d_pD"])
                        S.op("scalar", lambda e, nb=nb, f=f: e.activation(
                            out=bbc[nb][:, f * PSF:(f + 1) * PSF],
                            in_=psD[:], func=AF.Identity),
                            ins=["d_pD"], outs=[bbk])
                        S.op("tensor", lambda e, n=n, f=f, cb_=cb_: e.matmul(
                            out=psE[:],
                            lhsT=sel_t[:, (DS + n) * 128:(DS + n + 1) * 128],
                            rhs=bc32b[cb_][:, f * PSF:(f + 1) * PSF],
                            start=True, stop=True),
                            ins=[f"d_bc32b{cb_}", "selt"], outs=["d_pE"])
                        S.op("scalar", lambda e, nb=nb, f=f: e.activation(
                            out=cbc[nb][:, f * PSF:(f + 1) * PSF],
                            in_=psE[:], func=AF.Identity),
                            ins=["d_pE"], outs=[cbk])
                    for dc in range(NDC):
                        tb = (n * NDC + dc) % 2
                        ak, uk2 = f"d_dA{tb}", f"d_dBu{tb}"
                        sok, cek = f"d_so{tb}", f"d_ce{tb}"
                        S.op("scalar", lambda e, dc=dc, n=n, tb=tb, cb_=cb_:
                             e.activation(
                                 out=dA_[tb][:], in_=delta[dc][cb_][:],
                                 func=AF.Exp,
                                 scale=negA_t[:, dc * DS + n:dc * DS + n + 1]),
                            ins=[f"d_del{dc}_{cb_}", "negA"], outs=[ak])
                        S.op("vector", lambda e, dc=dc, tb=tb, nb=nb, cb_=cb_:
                             e.tensor_tensor(
                                 out=dBu_[tb][:], in0=du_b[dc][cb_][:],
                                 in1=bbc[nb][:], op=ALU.mult),
                             ins=[f"d_du{dc}_{cb_}", bbk], outs=[uk2])
                        S.op("vector", lambda e, dc=dc, n=n, tb=tb:
                             e.tensor_tensor_scan(
                                 out=so_[tb][:], data0=dA_[tb][:],
                                 data1=dBu_[tb][:],
                                 initial=state_t[:, dc * DS + n:
                                                 dc * DS + n + 1],
                                 op0=ALU.mult, op1=ALU.add),
                             ins=[ak, uk2, "state"], outs=[sok])
                        S.op("vector", lambda e, dc=dc, n=n, tb=tb:
                             e.tensor_copy(
                                 out=state_t[:, dc * DS + n:dc * DS + n + 1],
                                 in_=so_[tb][:, LC - 1:LC]),
                             ins=[sok], outs=["state"])
                        S.op("vector", lambda e, tb=tb, nb=nb: e.tensor_tensor(
                            out=so_[tb][:], in0=so_[tb][:], in1=cbc[nb][:],
                            op=ALU.mult), ins=[sok, cbk], outs=[sok])
                        if n == 0:
                            S.op("vector", lambda e, dc=dc, tb=tb:
                                 e.tensor_copy(
                                     out=yacc[dc][:], in_=so_[tb][:]),
                                 ins=[sok], outs=[f"d_y{dc}"])
                        else:
                            S.op("vector", lambda e, dc=dc, tb=tb:
                                 e.tensor_tensor(
                                     out=yacc[dc][:], in0=yacc[dc][:],
                                     in1=so_[tb][:], op=ALU.add),
                                 ins=[sok], outs=[f"d_y{dc}"])
                blk = g0 // TOKC
                colg = g0 % TOKC
                for dc in range(NDC):
                    ygb = so_[0]
                    S.op("vector", lambda e, dc=dc, cb_=cb_, ygb=ygb:
                         e.tensor_scalar(
                             out=ygb[:], in0=u_b[dc][cb_][:],
                             scalar1=D_t[:, dc:dc + 1], scalar2=None,
                             op0=ALU.mult),
                        ins=[f"p1u{dc}_{cb_}", "Dt"], outs=["d_so0"])
                    S.op("vector", lambda e, dc=dc, ygb=ygb: e.tensor_tensor(
                        out=ygb[:], in0=ygb[:], in1=yacc[dc][:], op=ALU.add),
                        ins=[f"d_y{dc}"], outs=["d_so0"])
                    S.op("vector", lambda e, dc=dc, cb_=cb_, ygb=ygb:
                         e.tensor_tensor(
                             out=ygb[:], in0=ygb[:], in1=sz_b[dc][cb_][:],
                             op=ALU.mult),
                        ins=[f"p1sz{dc}_{cb_}"], outs=["d_so0"])
                    S.dma("sync",
                          yg_a2ai[blk * DCORE + dc * 128:
                                  blk * DCORE + (dc + 1) * 128,
                                  colg:colg + LC],
                          ygb[:], ins=["d_so0"], outs=["dram_ygi"], sem="dst")

            NFU = NFH * HC

            def emit_ffn(u):
                fh = u // HC
                h_c = u % HC
                if h_c == 0:
                    qq = (fh * FHT) // QTOK
                    qc = (fh * FHT) % QTOK
                    for k in range(KC):
                        S.dma("sync", xnf[k][:],
                              xn_agi[qq * DM + k * 128:qq * DM + (k + 1) * 128,
                                     qc:qc + FHT],
                              ins=[f"dram_xnagi{qq}"], outs=[f"fxn{k}"], sem="dx")
                wb = h_c % 2
                wk = f"fw{wb}"
                a_r = h_c % NC_
                s_s = h_c // NC_
                S.dma("sync", w12[wb][:, 0:KC * 128],
                      w1_ago[a_r * 128:(a_r + 1) * 128,
                             s_s * KC * 128:(s_s + 1) * KC * 128],
                      ins=["dram_wago0"], outs=[wk], sem="dw")
                S.dma("sync", w12[wb][:, KC * 128:2 * KC * 128],
                      w2_ago[a_r * 128:(a_r + 1) * 128,
                             s_s * KC * 128:(s_s + 1) * KC * 128],
                      ins=["dram_wago1"], outs=[wk], sem="dw")
                for f in range(NFF):
                    fb = f % 2
                    psx = psF1 if fb == 0 else psF2
                    S.op("tensor", lambda e, wb=wb, f=f, psx=psx: [
                        e.matmul(out=psx[:],
                                 lhsT=w12[wb][:, k * 128:(k + 1) * 128],
                                 rhs=xnf[k][:, f * PSF:(f + 1) * PSF],
                                 start=(k == 0), stop=(k == KC - 1))
                        for k in range(KC)][-1],
                        ins=[wk] + [f"fxn{k}" for k in range(KC)],
                        outs=[f"fps{fb}"], sem="pe")
                    S.op("scalar", lambda e, fb=fb, psx=psx: e.activation(
                        out=tb_[fb][:], in_=psx[:], func=AF.Tanh,
                        scale=0.5), ins=[f"fps{fb}"], outs=[f"ftb{fb}"])
                    S.op("tensor", lambda e, wb=wb, f=f: [
                        e.matmul(out=psF3[:],
                                 lhsT=w12[wb][:, (KC + k) * 128:
                                              (KC + k + 1) * 128],
                                 rhs=xnf[k][:, f * PSF:(f + 1) * PSF],
                                 start=(k == 0), stop=(k == KC - 1))
                        for k in range(KC)][-1],
                        ins=[wk] + [f"fxn{k}" for k in range(KC)],
                        outs=["fps2"], sem="pe")
                    S.op("scalar", lambda e, fb=fb: e.activation(
                        out=x2h[fb][:], in_=psF3[:], func=AF.Identity,
                        scale=0.5), ins=["fps2"], outs=[f"fx2h{fb}"])
                    S.op("gpsimd", lambda e, fb=fb: e.tensor_scalar(
                        out=hb_[fb][:], in0=tb_[fb][:], scalar1=1.0,
                        scalar2=None, op0=ALU.add),
                        ins=[f"ftb{fb}"], outs=[f"fhb{fb}"])
                    S.op("gpsimd", lambda e, fb=fb: e.tensor_tensor(
                        out=hb_[fb][:], in0=hb_[fb][:], in1=x2h[fb][:],
                        op=ALU.mult),
                        ins=[f"fx2h{fb}"], outs=[f"fhb{fb}"])
                    S.dma("sync",
                          hid_dram[h_c * 128:(h_c + 1) * 128,
                                   fh * FHT + f * PSF:fh * FHT + (f + 1) * PSF],
                          hb_[fb][:], ins=[f"fhb{fb}"], outs=["dram_hid"],
                          sem="dst")

            emit_p1_start(0)
            for m in range(2 * NDC):
                emit_p1_m(0, m)
            emit_p1_conv(0)
            emit_d_load(0)
            fu = 0
            SKIP0 = min(3, NCH - 1)
            per = ceil_div(NFU, max(1, NCH - 1 - SKIP0))
            emit_d_head(0)
            for c in range(NCH):
                units = []
                if c + 1 < NCH:
                    emit_p1_start(c + 1)
                    for m in range(2 * NDC):
                        units.append(lambda m=m: emit_p1_m(c + 1, m))
                    units.append(lambda: emit_p1_conv(c + 1))
                    units.append(lambda: emit_d_load(c + 1))
                    units.append(lambda: emit_d_head(c + 1))
                stop = min(NFU, max(0, (c + 1 - SKIP0)) * per)
                while fu < stop:
                    units.append(lambda fu=fu: emit_ffn(fu))
                    fu += 1

                def inject(n, units=units):
                    k0 = len(units) * n // DS
                    k1 = len(units) * (n + 1) // DS
                    for k in range(k0, k1):
                        units[k]()
                emit_d_scan(c, inject)
            while fu < NFU:
                emit_ffn(fu)
                fu += 1

            with allow_ici_all_to_all():
                a2a_tok = S.op("gpsimd", lambda e: e.collective_compute(
                    "AllToAll", ALU.bypass, replica_groups=rg,
                    ins=[yg_a2ai.opt()], outs=[yg_a2ao.opt()]),
                    ins=["dram_ygi"], outs=["dram_ygo"], sem="cc",
                    extra=[S.current("dst")])

        S.fence()
        with contextlib.ExitStack() as ph:
            def sbW(name, shape, dt):
                return ph.enter_context(nc.sbuf_tensor(name, shape, dt))
            psum_ = mk_psum(ph)
            NBK = NC_ * NSLOT
            w3t = [sbW(f"f_w3{i}", [128, NBK * 128], BF16) for i in range(2)]
            x3b = [sbW(f"f_x3{i}", [128, PSF], F32) for i in range(2)]
            ps_f = [psum_(f"w3_ps{i}", [128, PSF], F32) for i in range(2)]

            def blk_of(kh):
                return (kh % NC_) * NSLOT + kh // NC_
            hid_all = sbW("f_hall", [128, HC * PSF], BF16)
            for fq in range(TOKC // PSF):
                hv = bass.AP(hid_dram.tensor, fq * PSF,
                             [[TOKC, 128], [128 * TOKC, HC], [1, PSF]])
                S.dma("sync", hid_all[:], hv, ins=["dram_hid"],
                      outs=["fhall"], sem="dx")
                for m in range(MDM):
                    wb = m % 2
                    wk = f"fw3{wb}"
                    w3v = bass.AP(w3_ago.tensor, m * NSLOT * 128,
                                  [[L3R, 128], [128 * L3R, NC_],
                                   [1, NSLOT * 128]])
                    S.dma("sync", w3t[wb][:, 0:NBK * 128], w3v,
                          ins=["dram_wago2"], outs=[wk], sem="dw")
                    pb = m % 2
                    S.op("tensor", lambda e, wb=wb, pb=pb: [
                        e.matmul(out=ps_f[pb][:],
                                 lhsT=w3t[wb][:, blk_of(kh) * 128:
                                              blk_of(kh) * 128 + 128],
                                 rhs=hid_all[:, kh * PSF:(kh + 1) * PSF],
                                 start=(kh == 0),
                                 stop=(kh == HC - 1)) for kh in range(HC)][-1],
                        ins=[wk, "fhall"],
                        outs=[f"w3ps{pb}"], sem="pe")
                    S.op("scalar", lambda e, pb=pb: e.activation(
                        out=x3b[pb][:], in_=ps_f[pb][:], func=AF.Identity),
                        ins=[f"w3ps{pb}"], outs=[f"fx3{pb}"])
                    S.dma("sync", x3_dram[m * 128:(m + 1) * 128,
                                          fq * PSF:(fq + 1) * PSF],
                          x3b[pb][:], ins=[f"fx3{pb}"], outs=["dram_x3"],
                          sem="dst")

        S.fence()
        with contextlib.ExitStack() as ph:
            def sb7(name, shape, dt):
                return ph.enter_context(nc.sbuf_tensor(name, shape, dt))
            psum_ = mk_psum(ph)
            FQ = 1024 if TOKC % 1024 == 0 else PSF
            NQ = TOKC // FQ
            NPF = FQ // PSF
            ygt_all = sb7("s_ygall", [128, KDI * FQ], BF16)
            wot = [sb7(f"s_wo{i}", [128, KDI * 128], BF16) for i in range(2)]
            x3t = [sb7(f"s_x3{i}", [128, FQ], F32) for i in range(2)]
            smf = [sb7(f"s_smf{i}", [128, FQ], F32) for i in range(2)]
            outb = [sb7(f"s_out{t}", [128, DM], F32) for t in range(FQ // 128)]
            resb = sb7("s_res", [128, DM], F32)
            ps_o = [psum_(f"s_pso{i}", [128, PSF], F32) for i in range(2)]
            ps_t7 = [psum_(f"s_pst{i}", [128, PSF], F32) for i in range(2)]

            for fq in range(NQ):
                ygv = bass.AP(yg_a2ao.tensor, fq * FQ,
                              [[TOKC, 128], [128 * TOKC, KDI], [1, FQ]])
                S.dma("sync", ygt_all[:], ygv, ins=["dram_ygo"],
                      outs=["sygall"], extra=[a2a_tok], sem="dx")
                for m in range(MDM):
                    wb = m % 2
                    wk = f"swo{wb}"
                    wov = bass.AP(wo_ago.tensor, m * NDD * 128,
                                  [[LOR, 128], [128 * LOR, NC_], [1, NDD * 128]])
                    S.dma("sync", wot[wb][:, 0:KDI * 128], wov,
                          ins=["dram_woago"], outs=[wk], sem="dw")
                    xk3 = f"sx3{m % 2}"
                    S.dma("sync", x3t[m % 2][:],
                          x3_dram[m * 128:(m + 1) * 128, fq * FQ:(fq + 1) * FQ],
                          ins=["dram_x3"], outs=[xk3], sem="dld")
                    for f2 in range(NPF):
                        pb = f2 % 2
                        S.op("tensor", lambda e, wb=wb, f2=f2, fq=fq, pb=pb: [
                            e.matmul(out=ps_o[pb][:],
                                     lhsT=wot[wb][:, k * 128:(k + 1) * 128],
                                     rhs=ygt_all[:, k * FQ + f2 * PSF:
                                                 k * FQ + (f2 + 1) * PSF],
                                     start=(k == 0), stop=(k == KDI - 1))
                            for k in range(KDI)][-1],
                            ins=[wk, "sygall"],
                            outs=[f"spo{pb}"], sem="pe")
                        S.op("vector", lambda e, pb=pb, f2=f2, m=m:
                             e.tensor_tensor(
                                 out=smf[m % 2][:, f2 * PSF:(f2 + 1) * PSF],
                                 in0=ps_o[pb][:],
                                 in1=x3t[m % 2][:, f2 * PSF:(f2 + 1) * PSF],
                                 op=ALU.add),
                             ins=[f"spo{pb}", xk3], outs=[f"ssmf{m % 2}"])
                    for tg in range(FQ // PSF):
                        pb = tg % 2
                        pk = f"spt{pb}"
                        for j in range(PSF // 128):
                            tt = tg * (PSF // 128) + j
                            S.op("tensor", lambda e, tt=tt, j=j, pb=pb, m=m:
                                 e.transpose(
                                     out=ps_t7[pb][:, j * 128:(j + 1) * 128],
                                     in_=smf[m % 2][:, tt * 128:(tt + 1) * 128],
                                     identity=eyef_t[:]),
                                 ins=[f"ssmf{m % 2}", "eyef"], outs=[pk])
                        for j in range(PSF // 128):
                            tt = tg * (PSF // 128) + j
                            S.op("vector", lambda e, tt=tt, j=j, pb=pb, m=m:
                                 e.tensor_copy(
                                     out=outb[tt][:, m * 128:(m + 1) * 128],
                                     in_=ps_t7[pb][:, j * 128:(j + 1) * 128]),
                                 ins=[pk], outs=[f"sout{tt}"])
                for t in range(FQ // 128):
                    row0 = fq * FQ + t * 128
                    S.dma("sync", resb[:], x_tok[row0:row0 + 128, :],
                          outs=["sres"])
                    S.op("vector", lambda e, t=t: e.tensor_tensor(
                        out=outb[t][:], in0=outb[t][:], in1=resb[:], op=ALU.add),
                        ins=["sres", f"sout{t}"], outs=[f"sout{t}"])
                    S.dma("sync", out_ext[row0:row0 + 128, :], outb[t][:],
                          ins=[f"sout{t}"], outs=["dram_out"], sem="dst")

            if debug:
                for nm, (ap_out, src) in dbg.items():
                    S.dma("sync", ap_out, src, ins=[], outs=[],
                          extra=[S.current("dst"), S.current("cc"),
                                 S.current("dgp")], sem="dst")

            ftok = S.publish("dst", S.cnt["dst"])
            S.op("gpsimd", lambda e, v=ftok[1]: e.wait_ge(S.sems["dst_r"], v),
                 sem=None)

        with allow_ici_all_to_all(), nc.Block() as block:
            @block.sync
            def _(eng):
                for f in S.q["sync"]:
                    f(eng)

            @block.tensor
            def _(eng):
                for f in S.q["tensor"]:
                    f(eng)

            @block.vector
            def _(eng):
                for f in S.q["vector"]:
                    f(eng)

            @block.scalar
            def _(eng):
                for f in S.q["scalar"]:
                    f(eng)

            @block.gpsimd
            def _(eng):
                for f in S.q["gpsimd"]:
                    f(eng)

    return nc


def make_inputs(cfg, full_inputs):
    B, L, DM, DI = cfg["B"], cfg["L"], cfg["DM"], cfg["DI"]
    DS, DTR, HFF = cfg["DS"], cfg["DTR"], cfg["HFF"]
    TOK = B * L
    TOKC = TOK // NC_
    DCORE = DI // NC_
    HC = HFF // 128
    NSLOT = ceil_div(HC, NC_)
    fi = full_inputs
    x = np.ascontiguousarray(fi["hidden"].reshape(TOK, DM))
    w_in = fi["in_proj"]
    eye_b = np.eye(128, dtype=ml_dtypes.bfloat16)
    DS_ = cfg["DS"]
    sel = np.zeros((2 * DS_, 2 * DS_ * 128), dtype=ml_dtypes.bfloat16)
    for j_ in range(2 * DS_):
        sel[j_, j_ * 128:(j_ + 1) * 128] = 1
    eye_f = np.eye(128, dtype=np.float32)
    ones_b = np.ones((1, 128), dtype=ml_dtypes.bfloat16)
    ones_fr = np.ones((1, 128), dtype=np.float32)
    ones_fc = np.ones((128, 1), dtype=np.float32)
    maps = []
    for i in range(NC_):
        dsl = slice(i * DCORE, (i + 1) * DCORE)
        chunks = [c for c in range(HC) if c % NC_ == i]
        w1s = np.zeros((DM, NSLOT * 128), np.float32)
        w2s = np.zeros((DM, NSLOT * 128), np.float32)
        w3s = np.zeros((NSLOT * 128, DM), np.float32)
        for s, c in enumerate(chunks):
            w1s[:, s * 128:(s + 1) * 128] = fi["w1"][c * 128:(c + 1) * 128, :].T
            w2s[:, s * 128:(s + 1) * 128] = fi["w2"][c * 128:(c + 1) * 128, :].T
            w3s[s * 128:(s + 1) * 128, :] = fi["w3"][:, c * 128:(c + 1) * 128].T
        maps.append(dict(
            x_tok=np.ascontiguousarray(x[i * TOKC:(i + 1) * TOKC]),
            winT_x=np.ascontiguousarray(w_in[dsl].T),
            winT_z=np.ascontiguousarray(w_in[DI + i * DCORE:
                                             DI + (i + 1) * DCORE].T),
            convw=np.ascontiguousarray(fi["conv_w"][dsl, 0, :]),
            convb=np.ascontiguousarray(fi["conv_b"][dsl, None]),
            negA_in=np.ascontiguousarray(fi["negA"][dsl]),
            D_in=np.ascontiguousarray(fi["D"][dsl, None]),
            dtb_in=np.ascontiguousarray(fi["dt_b"][dsl, None]),
            dtwT=np.ascontiguousarray(fi["dt_w"][dsl].T),
            xpwT=np.ascontiguousarray(fi["xp_w"][:, dsl].T),
            woT_sl=np.ascontiguousarray(fi["out_w"][:, dsl].T),
            w1T_sl=w1s, w2T_sl=w2s, w3T_sl=w3s,
            ones_bf=ones_b, ones_fr=ones_fr, ones_fc=ones_fc,
            eye_bf=eye_b, eye_f=eye_f, sel_in=sel,
        ))
    return maps


def kernel(L0_attn_hidden_states, L0_attn_in_proj_w, L0_attn_conv_w,
           L0_attn_conv_b, L0_attn_x_proj_w, L0_attn_dt_proj_w,
           L0_attn_dt_proj_b, L0_attn_out_proj_w, L0_attn_negA, L0_attn_D,
           L0_ffn_w1, L0_ffn_w2, L0_ffn_w3):
    cfg = FULL
    fi = dict(hidden=np.asarray(L0_attn_hidden_states),
              in_proj=np.asarray(L0_attn_in_proj_w),
              conv_w=np.asarray(L0_attn_conv_w),
              conv_b=np.asarray(L0_attn_conv_b),
              xp_w=np.asarray(L0_attn_x_proj_w),
              dt_w=np.asarray(L0_attn_dt_proj_w),
              dt_b=np.asarray(L0_attn_dt_proj_b),
              out_w=np.asarray(L0_attn_out_proj_w),
              negA=np.asarray(L0_attn_negA),
              D=np.asarray(L0_attn_D),
              w1=np.asarray(L0_ffn_w1),
              w2=np.asarray(L0_ffn_w2),
              w3=np.asarray(L0_ffn_w3))
    nc = build_program(cfg)
    in_maps = make_inputs(cfg, fi)
    res = run_bass_kernel_spmd(nc, in_maps, core_ids=list(range(NC_)))
    out = np.concatenate([np.asarray(r["out_sl"]) for r in res.results], axis=0)
    B, L, DM = cfg["B"], cfg["L"], cfg["DM"]
    return out.reshape(B, L, DM).astype(np.float32)
